# version 10
# baseline (speedup 1.0000x reference)
import contextlib
import numpy as np
import ml_dtypes
import concourse.bass as bass
import concourse.mybir as mybir
from concourse.bass_utils import run_bass_kernel_spmd

F32 = mybir.dt.float32
BF16 = mybir.dt.bfloat16
AF = mybir.ActivationFunctionType
ALU = mybir.AluOpType
AX = mybir.AxisListType

B, T, C, H, HD, FF = 4, 4096, 2048, 16, 128, 8192
NCORE = 8
TOK = 2048
CC = 16
NG = 4
BFD = ml_dtypes.bfloat16



class _Sem:
    def __init__(self, name):
        self.name = name
        self.n = 0
        self.h = None


class _MockInstr:
    __slots__ = ()
    def then_inc(self, sem, k=1):
        sem.n += k
        return self

_MOCK_INSTR = _MockInstr()


class _MockEngine:
    def wait_ge(self, sem, v):
        pass
    def __getattr__(self, name):
        return lambda *a, **k: _MOCK_INSTR


class _RealInstr:
    __slots__ = ("ins",)
    def __init__(self, ins):
        self.ins = ins
    def then_inc(self, sem, k=1):
        sem.n += k
        self.ins.then_inc(sem.h, k)
        return self


class _RealEngine:
    def __init__(self, eng):
        self._eng = eng
    def wait_ge(self, sem, v):
        self._eng.wait_ge(sem.h, v)
    def __getattr__(self, name):
        eng = self._eng
        def op(*a, **k):
            return _RealInstr(getattr(eng, name)(*a, **k))
        return op


class Env:
    def __init__(self):
        self.sems = {n: _Sem(n) for n in ("mm", "sc", "vc")}
        self.marks = {}
        self.mock = True
        self.eng = {}

    def S(self, name):
        if name not in self.sems:
            assert self.mock, f"new sem {name} in real pass"
            self.sems[name] = _Sem(name)
        return self.sems[name]

    def reset(self):
        for s in self.sems.values():
            s.n = 0

    def mark(self, name, semname):
        s = self.sems[semname]
        if self.mock:
            self.marks[name] = (semname, s.n)
        else:
            assert self.marks[name] == (semname, s.n), (
                name, self.marks[name], (semname, s.n))

    def wait(self, engname, markname):
        if self.mock:
            return
        semname, v = self.marks[markname]
        self.eng[engname].wait_ge(self.sems[semname], v)



def _rope_tables():
    p = np.arange(HD // 2, dtype=np.float32)
    theta = np.power(np.float32(10000.0),
                     (-(np.float32(2.0) ** p) / np.float32(HD)).astype(np.float32),
                     dtype=np.float32)
    pos = np.arange(1, T + 1, dtype=np.float32)
    th2 = np.concatenate([theta, theta]).astype(np.float32)
    ang = (pos[:, None] * th2[None, :]).astype(np.float32)
    return (np.ascontiguousarray(np.cos(ang).T),
            np.ascontiguousarray(np.sin(ang).T))


def _host_prep(L0_attn_x, L0_attn_Wqkv, L0_attn_bqkv, L0_attn_Wo, L0_attn_bo,
               L0_ffn_w_gate, L0_ffn_w_up, L0_ffn_w_down):
    f32 = np.float32
    x = np.asarray(L0_attn_x, f32)
    Wqkv = np.asarray(L0_attn_Wqkv, f32)
    bqkv = np.asarray(L0_attn_bqkv, f32)
    Wo = np.asarray(L0_attn_Wo, f32)
    bo = np.asarray(L0_attn_bo, f32)
    wg = np.asarray(L0_ffn_w_gate, f32)
    wu = np.asarray(L0_ffn_w_up, f32)
    wd = np.asarray(L0_ffn_w_down, f32)

    perm = np.concatenate([np.arange(s, FF, 4) for s in range(4)])

    wqkvT = np.ascontiguousarray(Wqkv.T)
    wqk = np.ascontiguousarray(
        wqkvT[:, :2 * C].reshape(CC, 128, 32, 128).transpose(2, 1, 0, 3)
        .reshape(32, 128, CC * 128))
    wv = np.ascontiguousarray(
        wqkvT[:, 2 * C:].reshape(CC, 128, 4, 512).transpose(2, 1, 0, 3)
        .reshape(4, 128, CC * 512))
    wo_t = np.ascontiguousarray(
        Wo.T.reshape(CC, 128, CC, 128).transpose(2, 1, 0, 3)
        .reshape(CC, 128, CC * 128))
    wg_t = np.ascontiguousarray(
        wg[:, perm].reshape(CC, 128, 64, 128).transpose(2, 1, 0, 3)
        .reshape(64, 128, CC * 128))
    wu_t = np.ascontiguousarray(
        wu[:, perm].reshape(CC, 128, 64, 128).transpose(2, 1, 0, 3)
        .reshape(64, 128, CC * 128))
    wd_t = np.ascontiguousarray(
        wd[perm, :].reshape(64, 128, CC, 128).transpose(2, 1, 0, 3)
        .reshape(CC, 128, 64 * 128))

    bqk_cols = np.ascontiguousarray(bqkv[:2 * C].reshape(32, 128).T)
    bv_rep = np.ascontiguousarray(np.broadcast_to(bqkv[2 * C:], (128, C)))
    bo_cols = np.ascontiguousarray(bo.reshape(CC, 128).T)

    cosT, sinT = _rope_tables()
    ident = np.eye(128, dtype=f32)
    ones1 = np.ones((1, 128), f32)
    onesk = np.ones((128, 1), f32)
    onesm = np.ones((128, 128), f32)
    minus1 = -np.ones((1, 128), f32)

    shared = dict(WQK=wqk, WV=wv, WO=wo_t, WG=wg_t, WU=wu_t, WD=wd_t,
                  BQK=bqk_cols, BV=bv_rep, BO=bo_cols,
                  COS=cosT, SIN=sinT,
                  IDN=ident, ONES1=ones1, ONESK=onesk, ONESM=onesm,
                  MINUS1=minus1)

    in_maps = []
    own_idx_all = []
    for core in range(NCORE):
        b, j = core // 2, core % 2
        starts = np.arange(16) * 256 + 128 * j
        own_idx = (starts[:, None] + np.arange(128)).ravel()
        own_idx_all.append(own_idx)
        xTb = np.ascontiguousarray(x[b].T)
        xTq = np.ascontiguousarray(x[b].T[:, own_idx])
        scale = np.float32(1.0) / np.sqrt(np.float32(HD))
        cosq = np.ascontiguousarray(cosT[:, own_idx] * scale)
        sinq = np.ascontiguousarray(sinT[:, own_idx] * scale)
        q_rel = np.arange(512)
        posq = 256 * (q_rel // 128) + 128 * j + (q_rel % 128)
        kv_rel = np.arange(1024)
        m = (posq[None, :] >= kv_rel[:, None])
        maskPT = np.ascontiguousarray(
            m.reshape(8, 128, 512).transpose(1, 0, 2)
            .reshape(128, 8 * 512).astype(BFD))
        in_maps.append(dict(shared, XT=xTb, XTQ=xTq, COSQ=cosq, SINQ=sinq,
                            MASK=maskPT))
    return in_maps, own_idx_all



class Arena:
    def __init__(self, ap, nbytes):
        self.ap = ap
        self.nbytes = nbytes
        self.off = 0
        self.hi = 0
        self.stack = []

    def push(self):
        self.stack.append(self.off)

    def pop(self):
        self.hi = max(self.hi, self.off)
        self.off = self.stack.pop()

    def carve(self, dtype, cols, rows=128):
        size = 2 if dtype == BF16 else 4
        nb = cols * size
        off = (self.off + 63) & ~63
        self.off = off + nb
        assert self.off <= self.nbytes, (
            f"arena overflow: {self.off} > {self.nbytes}")
        v = self.ap[:, off:off + nb].bitcast(dtype)
        if rows != 128:
            v = v[0:rows, :]
        return v



def build_program(debug_scratch=False):
    nc = bass.Bass("TRN2", target_bir_lowering=False, debug=False)

    D = {}
    def din(name, shape, dtype=F32):
        D[name] = nc.dram_tensor(name, shape, dtype, kind="ExternalInput").ap()
    din("XT", [C, T]); din("XTQ", [C, TOK])
    din("WQK", [32, 128, 2048]); din("WV", [4, 128, 8192])
    din("WO", [CC, 128, 2048])
    din("WG", [64, 128, 2048]); din("WU", [64, 128, 2048])
    din("WD", [CC, 128, 8192])
    din("BQK", [128, 32]); din("BV", [128, C]); din("BO", [128, CC])
    din("COS", [128, T]); din("SIN", [128, T])
    din("COSQ", [128, TOK]); din("SINQ", [128, TOK])
    din("MASK", [128, 8 * 512], BF16)
    din("IDN", [128, 128]); din("ONES1", [1, 128]); din("ONESK", [128, 1])
    din("ONESM", [128, 128]); din("MINUS1", [1, 128])

    OUT = nc.dram_tensor("OUT", [CC, 128, TOK], F32, kind="ExternalOutput").ap()

    skind = "ExternalOutput" if debug_scratch else "Internal"
    QT_s = nc.dram_tensor("QT_s", [H, 128, TOK], BF16, kind=skind).ap()
    KT_s = nc.dram_tensor("KT_s", [H, 128, T], BF16, kind=skind).ap()
    V_s = nc.dram_tensor("V_s", [H, 128, 4096], BF16, kind=skind).ap()
    YT_s = nc.dram_tensor("YT_s", [H * NG, 128, 512], BF16, kind=skind).ap()
    LN1_s = nc.dram_tensor("LN1_s", [CC, 128, TOK], F32, kind=skind).ap()
    LNB_s = nc.dram_tensor("LNB_s", [CC, 128, TOK], BF16, kind="Internal").ap()
    WGB_s = nc.dram_tensor("WGB_s", [64, 128, 2048], BF16, kind="Internal").ap()
    WUB_s = nc.dram_tensor("WUB_s", [64, 128, 2048], BF16, kind="Internal").ap()
    DBGNB = (nc.dram_tensor("DBGNB", [128, 16], F32, kind=skind).ap()
             if debug_scratch else None)
    DBGNM = (nc.dram_tensor("DBGNM", [1, 32], F32, kind=skind).ap()
             if debug_scratch else None)

    env = Env()
    ctx = contextlib.ExitStack()
    with ctx:
        ARENA_BYTES = 205 * 1024
        arena_t = ctx.enter_context(
            nc.sbuf_tensor("arena", [128, ARENA_BYTES], mybir.dt.uint8))
        A = Arena(arena_t[:, :], ARENA_BYTES)

        pb = [ctx.enter_context(nc.psum_tensor(f"pb{i}", [128, 512], F32))[:, :]
              for i in range(8)]

        mm, sc, vc = (env.sems[n] for n in ("mm", "sc", "vc"))

        ident_bf = A.carve(BF16, 128)
        ones1_bf = A.carve(BF16, 128, rows=1)
        onesk_bf = A.carve(BF16, 1)
        ones1_f = A.carve(F32, 128, rows=1)
        onesk_f = A.carve(F32, 1)
        onesm_bf = A.carve(BF16, 128)
        minus1_f = A.carve(F32, 128, rows=1)
        bqk_sb = A.carve(F32, 32)
        bo_sb = A.carve(F32, CC)
        nstrip = A.carve(F32, 16 * 12, rows=1)
        nrmmax = A.carve(F32, 32, rows=1)
        bndq_row = A.carve(F32, 16, rows=1)
        bndk_row = A.carve(F32, 16, rows=1)
        bnd_row = A.carve(F32, 16, rows=1)
        negB_sb = A.carve(F32, 16)

        A.push()
        xt = A.carve(BF16, CC * 2048)
        xtq_bf = A.carve(BF16, CC * 1024)
        cos_sb = A.carve(F32, 2048)
        sin_sb = A.carve(F32, 2048)
        cosq_sb = A.carve(F32, 1024)
        sinq_sb = A.carve(F32, 1024)
        bv_sb = A.carve(F32, C)
        wslab = [A.carve(BF16, 2048) for _ in range(2)]
        wvsl = [A.carve(BF16, CC * 512) for _ in range(2)]
        kstage = [A.carve(F32, 512) for _ in range(4)]
        krot = [A.carve(F32, 512) for _ in range(2)]
        kt1 = [A.carve(F32, 512) for _ in range(2)]
        kt2 = [A.carve(F32, 512) for _ in range(2)]
        kbf = [A.carve(BF16, 512) for _ in range(4)]
        ksq = [A.carve(BF16, 512) for _ in range(3)]
        A.pop()

        A.push()
        mask_sb = A.carve(BF16, 8 * 512)
        qbuf = [A.carve(BF16, TOK) for _ in range(2)]
        kbuf = [A.carve(BF16, T) for _ in range(2)]
        vbuf = [A.carve(BF16, T) for _ in range(2)]
        ptb = [A.carve(BF16, 512) for _ in range(4)]
        ptm = [A.carve(BF16, 512) for _ in range(4)]
        recipS = [A.carve(F32, 512) for _ in range(2)]
        ytout = [A.carve(BF16, 512) for _ in range(2)]
        A.pop()

        A.push()
        ybuf = [A.carve(BF16, H * 512) for _ in range(2)]
        wos = [A.carve(BF16, 2048) for _ in range(2)]
        t1sb = A.carve(F32, CC * 512)
        t1a = [A.carve(F32, 512) for _ in range(2)]
        xq = [A.carve(F32, 512) for _ in range(3)]
        sqb = [A.carve(F32, 512) for _ in range(2)]
        st3 = A.carve(F32, 6 * 512, rows=1)
        a_sb = A.carve(F32, 512)
        b_sb = A.carve(F32, 512)
        w1b = [A.carve(F32, 512) for _ in range(2)]
        lnf = [A.carve(F32, 512) for _ in range(2)]
        lnb = [A.carve(BF16, 512) for _ in range(2)]
        A.pop()

        A.push()
        lnt = A.carve(BF16, CC * 512)
        hsp = A.carve(BF16, 64 * 512)
        wgb = [A.carve(BF16, 2048) for _ in range(2)]
        wub = [A.carve(BF16, 2048) for _ in range(2)]
        wdb = [A.carve(BF16, 4096) for _ in range(2)]
        hq = [A.carve(BF16, 512) for _ in range(4)]
        habs = [A.carve(BF16, 512) for _ in range(4)]
        dd = [A.carve(BF16, 512) for _ in range(6)]
        ee = [A.carve(BF16, 512) for _ in range(2)]
        kkm = [A.carve(BF16, 512) for _ in range(4)]
        sg = [A.carve(F32, 512) for _ in range(2)]
        t2sb = A.carve(F32, CC * 512)
        lnfb = [A.carve(F32, 512) for _ in range(3)]
        sq4 = [A.carve(F32, 512) for _ in range(2)]
        st4 = A.carve(F32, 6 * 512, rows=1)
        a4_sb = A.carve(F32, 512)
        b4_sb = A.carve(F32, 512)
        o2 = [A.carve(F32, 512) for _ in range(2)]
        A.pop()

        def p1_tiles(hx):
            tiles = []
            for j in range(16):
                for pg in (4 * hx, 4 * hx + 1, 4 * hx + 2, 4 * hx + 3):
                    tiles.append(("K", 16 + j, pg))
                for qt in (2 * hx, 2 * hx + 1):
                    tiles.append(("Q", j, qt))
            for dgi in range(4):
                for pc in range(16 * hx, 16 * hx + 16):
                    tiles.append(("V", dgi, pc))
            return tiles

        n_q1 = len(p1_tiles(0))
        g_tiles = sum((p1_tiles(hx) for hx in range(2)), [])
        NT1 = len(g_tiles)
        n_kq1 = 96

        def p1_qx(i):
            return i // n_q1

        def p1_slab_of(tile, hx):
            kind, a_, b_ = tile
            if kind == "V":
                return ("wv", a_, hx)
            return ("wqk", a_, hx)

        slab_seq = []
        for i, tile in enumerate(g_tiles):
            s = p1_slab_of(tile, p1_qx(i))
            if not slab_seq or slab_seq[-1] != s:
                slab_seq.append(s)
        slab_pos = {s: i for i, s in enumerate(slab_seq)}
        NSLAB = len(slab_seq)

        p2b_chunks = []
        for h_ in range(H):
            for u in range(NG):
                nch = 8 * (u + 1)
                for c_ in range(nch):
                    p2b_chunks.append((h_, u, c_, nch))
        NCH2B = len(p2b_chunks)

        kq_index = {}
        kq_slot = {}
        _kq = 0
        for _i, (_kind, _a, _b) in enumerate(g_tiles):
            if _kind in ("K", "Q"):
                kq_index[_i] = _kq
                if _kind == "Q":
                    kq_slot[_i] = _a * 12 + _b
                else:
                    kq_slot[_i] = (_a - 16) * 12 + 4 + _b
                _kq += 1
            else:
                kq_index[_i] = None
        NKQ = _kq


        def emit_gpsimd(g):
            env.eng["g"] = g
            for name, dst in (("IDN", ident_bf), ("ONES1", ones1_bf),
                              ("ONESK", onesk_bf), ("ONESM", onesm_bf)):
                g.dma_start(out=dst, in_=D[name]).then_inc(env.S("gcbf"), 16)
            env.mark("consts_bf", "gcbf")
            hx_loaded = -1
            for si, (kind, idx, hx) in enumerate(slab_seq):
                while hx_loaded < hx:
                    hn = hx_loaded + 1
                    if hn >= 1:
                        env.wait("g", f"p1.quarter.{hn - 1}.done")
                    sx = env.S("gxt")
                    for cc_ in range(CC):
                        g.dma_start(out=xt[:, cc_ * 2048:(cc_ + 1) * 2048],
                                    in_=D["XT"][cc_ * 128:(cc_ + 1) * 128,
                                                hn * 2048:(hn + 1) * 2048]
                                    ).then_inc(sx, 16)
                    for cc_ in range(CC):
                        g.dma_start(out=xtq_bf[:, cc_ * 1024:(cc_ + 1) * 1024],
                                    in_=D["XTQ"][cc_ * 128:(cc_ + 1) * 128,
                                                 hn * 1024:(hn + 1) * 1024]
                                    ).then_inc(sx, 16)
                    env.mark(f"p1.xt.{hn}", "gxt")
                    hx_loaded = hn
                if si >= 2:
                    env.wait("g", f"p1.slabfree.{si - 2}")
                src = D["WQK"][idx] if kind == "wqk" else D["WV"][idx]
                buf = wslab[si % 2] if kind == "wqk" else wvsl[si % 2]
                g.dma_start(out=buf, in_=src).then_inc(env.S(f"gsl{si % 2}"), 16)
                env.mark(f"p1.slab.{si}", f"gsl{si % 2}")
            for qd in range(16):
                for s_ in range(4):
                    ig = s_ * 16 + qd
                    j = qd * 4 + s_
                    gcv = env.S("gcv")
                    g.dma_start(out=WGB_s[ig], in_=D["WG"][ig]).then_inc(gcv, 16)
                    g.dma_start(out=WUB_s[ig], in_=D["WU"][ig]).then_inc(gcv, 16)
                    env.mark(f"p0.cv.{j}", "gcv")
            for g_ in range(NG):
                for cc_ in range(CC):
                    i = g_ * CC + cc_
                    if i == 0:
                        env.wait("g", "p2.alldone")
                    if i >= 2:
                        env.wait("g", f"p3.slabfree.{i - 2}")
                    g.dma_start(out=wos[i % 2], in_=D["WO"][cc_]
                                ).then_inc(env.S(f"gwo{i % 2}"), 16)
                    env.mark(f"p3.slab.{i}", f"gwo{i % 2}")
            for g_ in range(NG):
                for cc_ in range(CC):
                    for hh in range(2):
                        i = (g_ * CC + cc_) * 2 + hh
                        if i == 0:
                            env.wait("g", "p3.alldone")
                        if i >= 2:
                            env.wait("g", f"p4.wdfree.{i - 2}")
                        g.dma_start(out=wdb[i % 2],
                                    in_=D["WD"][cc_][:, hh * 4096:(hh + 1) * 4096]
                                    ).then_inc(env.S(f"gwd{i % 2}"), 16)
                        env.mark(f"p4.wd.{i}", f"gwd{i % 2}")

        def emit_sync(sy):
            env.eng["sy"] = sy
            yc = env.S("ycst")
            sy.dma_start(out=ones1_f, in_=D["ONES1"]).then_inc(yc, 16)
            sy.dma_start(out=onesk_f, in_=D["ONESK"]).then_inc(yc, 16)
            sy.dma_start(out=minus1_f, in_=D["MINUS1"]).then_inc(yc, 16)
            sy.dma_start(out=bqk_sb, in_=D["BQK"]).then_inc(yc, 16)
            sy.dma_start(out=bo_sb, in_=D["BO"]).then_inc(yc, 16)
            sy.dma_start(out=bv_sb, in_=D["BV"]).then_inc(yc, 16)
            env.mark("consts", "ycst")
            def tab_load(hn):
                yt_ = env.S("ytab")
                sy.dma_start(out=cos_sb,
                             in_=D["COS"][:, hn * 2048:(hn + 1) * 2048]
                             ).then_inc(yt_, 16)
                sy.dma_start(out=sin_sb,
                             in_=D["SIN"][:, hn * 2048:(hn + 1) * 2048]
                             ).then_inc(yt_, 16)
                sy.dma_start(out=cosq_sb,
                             in_=D["COSQ"][:, hn * 1024:(hn + 1) * 1024]
                             ).then_inc(yt_, 16)
                sy.dma_start(out=sinq_sb,
                             in_=D["SINQ"][:, hn * 1024:(hn + 1) * 1024]
                             ).then_inc(yt_, 16)
                env.mark(f"p1.tab.{hn}", "ytab")
            tab_load(0)
            for i, tile in enumerate(g_tiles):
                if i == n_kq1:
                    env.wait("sy", "p1.tabfree.0")
                    tab_load(1)
                kind, a_, b_ = tile
                env.wait("sy", f"p1.v.{i}")
                ys = env.S(f"ysp{i % 4}")
                if kind == "K":
                    sy.dma_start(out=KT_s[a_ - 16][:, b_ * 512:(b_ + 1) * 512],
                                 in_=kbf[i % 4]).then_inc(ys, 16)
                elif kind == "Q":
                    sy.dma_start(out=QT_s[a_][:, b_ * 512:(b_ + 1) * 512],
                                 in_=kbf[i % 4]).then_inc(ys, 16)
                else:
                    for hh in range(4):
                        sy.dma_start(out=V_s[a_ * 4 + hh][:, b_ * 128:(b_ + 1) * 128],
                                     in_=kbf[i % 4][:, hh * 128:(hh + 1) * 128]
                                     ).then_inc(ys, 16)
                env.mark(f"p1.spill.{i}", f"ysp{i % 4}")
            for dd_ in range(1, 5):
                env.wait("sy", f"p1.spill.{NT1 - dd_}")
            env.wait("sy", "p2.negB")
            if debug_scratch:
                sy.dma_start(out=DBGNB, in_=negB_sb[:, 0:16]
                             ).then_inc(env.S("ymsk"), 16)
                sy.dma_start(out=DBGNM, in_=nrmmax[0:1, 0:32]
                             ).then_inc(env.S("ymsk"), 16)
            sy.dma_start(out=mask_sb, in_=D["MASK"]).then_inc(env.S("ymsk"), 16)
            env.mark("p2.mask", "ymsk")
            for h_ in range(H + 1):
                if h_ < H:
                    if h_ >= 2:
                        env.wait("sy", f"p2b.head.{h_ - 2}.done")
                    yl = env.S(f"yld{h_ % 2}")
                    sy.dma_start(out=qbuf[h_ % 2], in_=QT_s[h_]).then_inc(yl, 16)
                    sy.dma_start(out=kbuf[h_ % 2], in_=KT_s[h_]).then_inc(yl, 16)
                    sy.dma_start(out=vbuf[h_ % 2], in_=V_s[h_]).then_inc(yl, 16)
                    env.mark(f"p2b.load.{h_}", f"yld{h_ % 2}")
                if h_ >= 1:
                    for u in range(NG):
                        i2 = (h_ - 1) * NG + u
                        env.wait("sy", f"p2b.yt.{i2}")
                        sy.dma_start(out=YT_s[i2], in_=ytout[i2 % 2]
                                     ).then_inc(env.S(f"yyt{i2 % 2}"), 16)
                        env.mark(f"p2b.ytspill.{i2}", f"yyt{i2 % 2}")
            env.wait("sy", "p2.alldone")
            for g_ in range(NG):
                if g_ >= 2:
                    env.wait("sy", f"p3.g.{g_ - 2}.done")
                env.wait("sy", f"p2b.ytspill.{60 + g_}")
                if 59 + g_ >= 0:
                    env.wait("sy", f"p2b.ytspill.{59 + g_}")
                yb = env.S(f"yyb{g_ % 2}")
                for h_ in range(H):
                    sy.dma_start(out=ybuf[g_ % 2][:, h_ * 512:(h_ + 1) * 512],
                                 in_=YT_s[h_ * NG + g_]).then_inc(yb, 16)
                env.mark(f"p3.ybuf.{g_}", f"yyb{g_ % 2}")
                for cc_ in range(CC):
                    i = g_ * CC + cc_
                    if i >= 3:
                        env.wait("sy", f"p3.xqfree.{i - 3}")
                    sy.dma_start(out=xq[i % 3],
                                 in_=D["XTQ"][cc_ * 128:(cc_ + 1) * 128,
                                              g_ * 512:(g_ + 1) * 512]
                                 ).then_inc(env.S(f"yxq{i % 3}"), 16)
                    env.mark(f"p3.xq.{i}", f"yxq{i % 3}")
                for cc_ in range(CC):
                    i = g_ * CC + cc_
                    env.wait("sy", f"p3.lnf.{i}")
                    sy.dma_start(out=LN1_s[cc_][:, g_ * 512:(g_ + 1) * 512],
                                 in_=lnf[i % 2]).then_inc(env.S(f"ylo{i % 2}"), 16)
                    env.mark(f"p3.lnfout.{i}", f"ylo{i % 2}")
                    env.wait("sy", f"p3.lnb.{i}")
                    sy.dma_start(out=LNB_s[cc_][:, g_ * 512:(g_ + 1) * 512],
                                 in_=lnb[i % 2]).then_inc(env.S(f"ylb{i % 2}"), 16)
                    env.mark(f"p3.lnbout.{i}", f"ylb{i % 2}")
            env.wait("sy", "p3.alldone")
            for g_ in range(NG):
                if g_ >= 1:
                    env.wait("sy", f"p4.gvec.{g_ - 1}.done")
                    env.wait("sy", f"p4.g.{g_ - 1}.done")
                for cc_ in range(CC):
                    sy.dma_start(out=lnt[:, cc_ * 512:(cc_ + 1) * 512],
                                 in_=LNB_s[cc_][:, g_ * 512:(g_ + 1) * 512]
                                 ).then_inc(env.S("ylnt"), 16)
                env.mark(f"p4.lnt.{g_}", "ylnt")
                for qd in range(16):
                    for s_ in range(4):
                        i = g_ * 64 + qd * 4 + s_
                        if i >= 2:
                            env.wait("sy", f"p4.gufree.{i - 2}")
                        ig = s_ * 16 + qd
                        env.wait("sy", f"p0.cv.{qd * 4 + s_}")
                        yg = env.S(f"ygu{i % 2}")
                        sy.dma_start(out=wgb[i % 2], in_=WGB_s[ig]
                                     ).then_inc(yg, 16)
                        env.mark(f"p4.gu.{i}", f"ygu{i % 2}")
                for cc_ in range(CC):
                    i = g_ * CC + cc_
                    if i >= 3:
                        env.wait("sy", f"p4.lnffree.{i - 3}")
                    sy.dma_start(out=lnfb[i % 3],
                                 in_=LN1_s[cc_][:, g_ * 512:(g_ + 1) * 512]
                                 ).then_inc(env.S(f"ylf{i % 3}"), 16)
                    env.mark(f"p4.lnfb.{i}", f"ylf{i % 3}")
                for cc_ in range(CC):
                    i = g_ * CC + cc_
                    env.wait("sy", f"p4.o2.{i}")
                    sy.dma_start(out=OUT[cc_][:, g_ * 512:(g_ + 1) * 512],
                                 in_=o2[i % 2]).then_inc(env.S(f"yo2{i % 2}"), 16)
                    env.mark(f"p4.o2out.{i}", f"yo2{i % 2}")
            env.wait("sy", f"p4.o2out.{NG * CC - 1}")
            env.wait("sy", f"p4.o2out.{NG * CC - 2}")

        def emit_tensor(t):
            env.eng["t"] = t
            env.wait("t", "consts")
            env.wait("t", "consts_bf")
            for i, tile in enumerate(g_tiles):
                kind, a_, b_ = tile
                qx = p1_qx(i)
                env.wait("t", f"p1.xt.{qx}")
                si = slab_pos[p1_slab_of(tile, qx)]
                env.wait("t", f"p1.slab.{si}")
                if i >= 4:
                    env.wait("t", f"p1.free.{i - 4}")
                bank = pb[i % 4]
                if kind == "K":
                    sl = wslab[si % 2]
                    pos0 = b_ * 512 - 2048 * qx
                    for cc_ in range(CC):
                        ins = t.matmul(
                            bank, sl[:, cc_ * 128:(cc_ + 1) * 128],
                            xt[:, cc_ * 2048 + pos0:cc_ * 2048 + pos0 + 512],
                            start=(cc_ == 0), stop=(cc_ == CC - 1))
                elif kind == "Q":
                    sl = wslab[si % 2]
                    posq = (b_ % 2) * 512
                    for cc_ in range(CC):
                        ins = t.matmul(
                            bank, sl[:, cc_ * 128:(cc_ + 1) * 128],
                            xtq_bf[:, cc_ * 1024 + posq:cc_ * 1024 + posq + 512],
                            start=(cc_ == 0), stop=(cc_ == CC - 1))
                else:
                    sl = wvsl[si % 2]
                    pos0 = b_ * 128 - 2048 * qx
                    for cc_ in range(CC):
                        ins = t.matmul(
                            bank,
                            xt[:, cc_ * 2048 + pos0:cc_ * 2048 + pos0 + 128],
                            sl[:, cc_ * 512:(cc_ + 1) * 512],
                            start=(cc_ == 0), stop=(cc_ == CC - 1))
                ins.then_inc(mm, 1)
                env.mark(f"p1.mm.{i}", "mm")
                nxt_si = (slab_pos[p1_slab_of(g_tiles[i + 1], p1_qx(i + 1))]
                          if i + 1 < NT1 else None)
                if nxt_si != si:
                    env.mark(f"p1.slabfree.{si}", "mm")
                if (i + 1) % n_q1 == 0:
                    env.mark(f"p1.quarter.{i // n_q1}.done", "mm")
                if i >= 1 and kq_index[i - 1] is not None:
                    kqp = kq_index[i - 1]
                    env.wait("t", f"p1.ksq.{kqp}")
                    if kqp >= 2:
                        env.wait("t", f"p1.nred.{kqp - 2}")
                    t.matmul(pb[4 + kqp % 2][0:1, :], onesk_bf, ksq[kqp % 3],
                             start=True, stop=True).then_inc(mm, 1)
                    env.mark(f"p1.nrm.{kqp}", "mm")
            env.wait("t", "p2.bndneg")
            t.matmul(pb[6][:, 0:16], minus1_f, bnd_row[0:1, 0:16],
                     start=True, stop=True).then_inc(mm, 1)
            env.mark("p2.bndmm", "mm")

            def p2b_tail(gj):
                hj, uj, cj, nchj = p2b_chunks[gj]
                i2j = hj * NG + uj
                band = (cj >= nchj - 8)
                if band:
                    env.wait("t", f"p2b.pt.{gj}")
                else:
                    env.wait("t", f"p2b.exps.{gj}")
                if cj == 0 and i2j >= 2:
                    env.wait("t", f"p2b.yt.{i2j - 2}")
                    env.wait("t", f"p2b.recip.{i2j - 2}")
                src = ptm[gj % 4] if band else ptb[gj % 4]
                t.matmul(pb[5 + i2j % 2], onesm_bf, src,
                         start=(cj == 0), stop=(cj == nchj - 1))
                t.matmul(pb[3 + i2j % 2],
                         vbuf[hj % 2][:, cj * 128:(cj + 1) * 128],
                         src, start=(cj == 0), stop=(cj == nchj - 1)
                         ).then_inc(mm, 1)
                env.mark(f"p2b.pv.{gj}", "mm")
                if cj == nchj - 1 and uj == NG - 1:
                    env.mark(f"p2b.head.{hj}.done", "mm")

            next_tail = [0]

            def flush_tails(n):
                while next_tail[0] < n:
                    p2b_tail(next_tail[0])
                    next_tail[0] += 1

            srot = (0, 1, 2, 7)
            for gi, (h_, u, c_, nch) in enumerate(p2b_chunks):
                if u == 0 and c_ == 0:
                    flush_tails(gi)
                    env.wait("t", f"p2b.load.{h_}")
                if gi >= 4:
                    env.wait("t", f"p2b.exps.{gi - 4}")
                t.matmul(pb[srot[gi % 4]],
                         kbuf[h_ % 2][:, c_ * 128:(c_ + 1) * 128],
                         qbuf[h_ % 2][:, u * 512:(u + 1) * 512],
                         start=True, stop=True).then_inc(mm, 1)
                env.mark(f"p2b.st.{gi}", "mm")
                flush_tails(gi - 1)
            flush_tails(NCH2B)
            for g_ in range(NG):
                env.wait("t", f"p3.ybuf.{g_}")
                for cc_ in range(CC):
                    i = g_ * CC + cc_
                    env.wait("t", f"p3.slab.{i}")
                    if i >= 2:
                        env.wait("t", f"p3.t1a.{i - 2}")
                    bank = pb[i % 2]
                    for h_ in range(H):
                        ins = t.matmul(
                            bank, wos[i % 2][:, h_ * 128:(h_ + 1) * 128],
                            ybuf[g_ % 2][:, h_ * 512:(h_ + 1) * 512],
                            start=(h_ == 0), stop=(h_ == H - 1))
                    ins.then_inc(mm, 1)
                    env.mark(f"p3.o.{i}", "mm")
                    env.mark(f"p3.slabfree.{i}", "mm")
                    env.wait("t", f"p3.t1.{i}")
                    if cc_ == 0 and g_ >= 1:
                        env.wait("t", f"p3.var.{g_ - 1}")
                    t.matmul(pb[2][0:1, :], onesk_f,
                             t1sb[:, cc_ * 512:(cc_ + 1) * 512],
                             start=(cc_ == 0), stop=(cc_ == CC - 1)
                             ).then_inc(mm, 1)
                    env.mark(f"p3.st1.{i}", "mm")
                    env.wait("t", f"p3.sq.{i}")
                    t.matmul(pb[3][0:1, :], onesk_f, sqb[i % 2],
                             start=(cc_ == 0), stop=(cc_ == CC - 1)
                             ).then_inc(mm, 1)
                    env.mark(f"p3.st2.{i}", "mm")
                env.wait("t", f"p3.ab.{g_}")
                t.matmul(pb[4], ones1_f, a_sb[0:1, :], start=True, stop=True)
                t.matmul(pb[5], ones1_f, b_sb[0:1, :], start=True, stop=True
                         ).then_inc(mm, 1)
                env.mark(f"p3.abbc.{g_}", "mm")
                env.wait("t", f"p3.absb.{g_}")
                env.mark(f"p3.g.{g_}.done", "mm")
            for g_ in range(NG):
                env.wait("t", f"p4.lnt.{g_}")
                for qd in range(16):
                    for s_ in range(4):
                        i = g_ * 64 + qd * 4 + s_
                        env.wait("t", f"p4.gu.{i}")
                        if i >= 2:
                            env.wait("t", f"p4.sg.{i - 2}")
                        gbank = pb[s_ % 2]
                        for cc_ in range(CC):
                            ins = t.matmul(
                                gbank, wgb[i % 2][:, cc_ * 128:(cc_ + 1) * 128],
                                lnt[:, cc_ * 512:(cc_ + 1) * 512],
                                start=(cc_ == 0), stop=(cc_ == CC - 1))
                        ins.then_inc(mm, 1)
                        env.mark(f"p4.gmm.{i}", "mm")
                        env.wait("t", f"p4.wub.{i}")
                        if i >= 2:
                            env.wait("t", f"p4.hqv.{i - 2}")
                        ubank = pb[2 + s_ % 2]
                        for cc_ in range(CC):
                            ins = t.matmul(
                                ubank, wub[i % 2][:, cc_ * 128:(cc_ + 1) * 128],
                                lnt[:, cc_ * 512:(cc_ + 1) * 512],
                                start=(cc_ == 0), stop=(cc_ == CC - 1))
                        ins.then_inc(mm, 1)
                        env.mark(f"p4.umm.{i}", "mm")
                        env.mark(f"p4.gufree.{i}", "mm")
                env.wait("t", f"p4.hsp.{g_}")
                for cc_ in range(CC):
                    i = g_ * CC + cc_
                    if i >= 2:
                        env.wait("t", f"p4.t2v.{i - 2}")
                    bank = pb[4 + i % 2]
                    for hh in range(2):
                        wi = i * 2 + hh
                        env.wait("t", f"p4.wd.{wi}")
                        for q_ in range(32):
                            ig = hh * 32 + q_
                            ins = t.matmul(
                                bank, wdb[wi % 2][:, q_ * 128:(q_ + 1) * 128],
                                hsp[:, ig * 512:(ig + 1) * 512],
                                start=(ig == 0), stop=(ig == 63))
                        ins.then_inc(mm, 1)
                        env.mark(f"p4.wdfree.{wi}", "mm")
                    env.mark(f"p4.dmm.{i}", "mm")
                    env.wait("t", f"p4.t2v.{i}")
                    if cc_ == 0 and g_ >= 1:
                        env.wait("t", f"p4.var.{g_ - 1}")
                    t.matmul(pb[6][0:1, :], onesk_f,
                             t2sb[:, cc_ * 512:(cc_ + 1) * 512],
                             start=(cc_ == 0), stop=(cc_ == CC - 1)
                             ).then_inc(mm, 1)
                    env.mark(f"p4.st1.{i}", "mm")
                    env.wait("t", f"p4.sq.{i}")
                    t.matmul(pb[0][0:1, :], onesk_f, sq4[i % 2],
                             start=(cc_ == 0), stop=(cc_ == CC - 1)
                             ).then_inc(mm, 1)
                    env.mark(f"p4.st2.{i}", "mm")
                env.wait("t", f"p4.ab.{g_}")
                t.matmul(pb[0], ones1_f, a4_sb[0:1, :], start=True, stop=True)
                t.matmul(pb[1], ones1_f, b4_sb[0:1, :], start=True, stop=True
                         ).then_inc(mm, 1)
                env.mark(f"p4.abbc.{g_}", "mm")
                env.wait("t", f"p4.absb.{g_}")
                env.mark(f"p4.g.{g_}.done", "mm")

        def emit_scalar(s):
            env.eng["s"] = s
            env.wait("s", "consts")
            prev_kq = None
            for i, tile in enumerate(g_tiles):
                kind, a_, b_ = tile
                if kind != "V":
                    env.wait("s", f"p1.mm.{i}")
                    if i >= 4:
                        env.wait("s", f"p1.v.{i - 4}")
                    s.add(kstage[i % 4], pb[i % 4], bqk_sb[:, a_:a_ + 1]
                          ).then_inc(sc, 1)
                    env.mark(f"p1.sc.{i}", "sc")
                if prev_kq is not None:
                    kqp = kq_index[prev_kq]
                    env.wait("s", f"p1.v.{prev_kq}")
                    if kqp >= 3:
                        env.wait("s", f"p1.nrm.{kqp - 3}")
                    s.activation(ksq[kqp % 3], kbf[prev_kq % 4], AF.Square
                                 ).then_inc(sc, 1)
                    env.mark(f"p1.ksq.{kqp}", "sc")
                    prev_kq = None
                if kind != "V":
                    prev_kq = i
            env.wait("s", "p2.nrm2")
            s.activation(bndq_row[0:1, 0:16], nrmmax[0:1, 0:16], AF.Sqrt,
                         scale=1.1025)
            s.activation(bndk_row[0:1, 0:16], nrmmax[0:1, 16:32], AF.Sqrt
                         ).then_inc(sc, 1)
            env.mark("p2.bndsq", "sc")
            for gi, (h_, u, c_, nch) in enumerate(p2b_chunks):
                if gi == 0:
                    env.wait("s", "p2.negB")
                env.wait("s", f"p2b.st.{gi}")
                if gi >= 4:
                    env.wait("s", f"p2b.pv.{gi - 4}")
                s.activation(ptb[gi % 4], pb[(0, 1, 2, 7)[gi % 4]], AF.Exp,
                             bias=negB_sb[:, h_:h_ + 1]).then_inc(sc, 1)
                env.mark(f"p2b.exps.{gi}", "sc")
            for g_ in range(NG):
                for cc_ in range(CC):
                    i = g_ * CC + cc_
                    env.wait("s", f"p3.o.{i}")
                    if i >= 2:
                        env.wait("s", f"p3.t1.{i - 2}")
                    s.add(t1a[i % 2], pb[i % 2], bo_sb[:, cc_:cc_ + 1]
                          ).then_inc(sc, 1)
                    env.mark(f"p3.t1a.{i}", "sc")
                    env.wait("s", f"p3.t1.{i}")
                    if i >= 2:
                        env.wait("s", f"p3.st2.{i - 2}")
                    s.activation(sqb[i % 2],
                                 t1sb[:, cc_ * 512:(cc_ + 1) * 512], AF.Square
                                 ).then_inc(sc, 1)
                    env.mark(f"p3.sq.{i}", "sc")
                env.wait("s", f"p3.var.{g_}")
                s.activation(st3[0:1, 2048:2560], st3[0:1, 1536:2048], AF.Ln
                             ).then_inc(sc, 1)
                s.activation(a_sb[0:1, 0:512], st3[0:1, 2048:2560], AF.Exp,
                             scale=-0.5).then_inc(sc, 1)
                env.mark(f"p3.rstd.{g_}", "sc")
                for cc_ in range(CC):
                    i = g_ * CC + cc_
                    env.wait("s", f"p3.lnf.{i}")
                    if i >= 2:
                        env.wait("s", f"p3.lnbout.{i - 2}")
                    s.copy(lnb[i % 2], lnf[i % 2]).then_inc(sc, 1)
                    env.mark(f"p3.lnb.{i}", "sc")
            def wub_load(iw):
                qd_, s2 = (iw % 64) // 4, iw % 4
                if iw >= 2:
                    env.wait("s", f"p4.gufree.{iw - 2}")
                env.wait("s", f"p0.cv.{qd_ * 4 + s2}")
                s.dma_start(out=wub[iw % 2], in_=WUB_s[s2 * 16 + qd_]
                            ).then_inc(env.S(f"sgu{iw % 2}"), 16)
                env.mark(f"p4.wub.{iw}", f"sgu{iw % 2}")
            env.wait("s", "p3.alldone")
            wub_load(0)
            wub_load(1)
            for g_ in range(NG):
                for qd in range(16):
                    for s_ in range(4):
                        i = g_ * 64 + qd * 4 + s_
                        if i + 2 < NG * 64:
                            wub_load(i + 2)
                        env.wait("s", f"p4.gmm.{i}")
                        if i >= 2:
                            env.wait("s", f"p4.sgfree.{i - 2}")
                        s.activation(sg[i % 2], pb[s_ % 2], AF.Silu
                                     ).then_inc(sc, 1)
                        env.mark(f"p4.sg.{i}", "sc")
                    for s_ in range(4):
                        i = g_ * 64 + qd * 4 + s_
                        env.wait("s", f"p4.hqv.{i}")
                        s.activation(habs[s_], hq[s_], AF.Abs).then_inc(sc, 1)
                    env.mark(f"p4.habs.{g_}.{qd}", "sc")
                for cc_ in range(CC):
                    i = g_ * CC + cc_
                    env.wait("s", f"p4.t2v.{i}")
                    if i >= 2:
                        env.wait("s", f"p4.st2.{i - 2}")
                    s.activation(sq4[i % 2],
                                 t2sb[:, cc_ * 512:(cc_ + 1) * 512], AF.Square
                                 ).then_inc(sc, 1)
                    env.mark(f"p4.sq.{i}", "sc")
                env.wait("s", f"p4.var.{g_}")
                s.activation(st4[0:1, 2048:2560], st4[0:1, 1536:2048], AF.Ln
                             ).then_inc(sc, 1)
                s.activation(a4_sb[0:1, 0:512], st4[0:1, 2048:2560], AF.Exp,
                             scale=-0.5).then_inc(sc, 1)
                env.mark(f"p4.rstd.{g_}", "sc")

        def emit_vector(v):
            env.eng["v"] = v
            env.wait("v", "consts")
            for i, tile in enumerate(g_tiles):
                kind, a_, b_ = tile
                qx = p1_qx(i)
                if i >= 4:
                    env.wait("v", f"p1.spill.{i - 4}")
                    if kq_index[i - 4] is not None:
                        env.wait("v", f"p1.ksq.{kq_index[i - 4]}")
                if kind == "V":
                    env.wait("v", f"p1.mm.{i}")
                    v.tensor_tensor(kbf[i % 4], pb[i % 4],
                                    bv_sb[:, a_ * 512:(a_ + 1) * 512], ALU.add
                                    ).then_inc(vc, 1)
                else:
                    env.wait("v", f"p1.sc.{i}")
                    env.wait("v", f"p1.tab.{qx}")
                    if kind == "K":
                        pos0 = b_ * 512 - 2048 * qx
                        cosv, sinv = cos_sb, sin_sb
                    else:
                        pos0 = (b_ % 2) * 512
                        cosv, sinv = cosq_sb, sinq_sb
                    ks = kstage[i % 4]
                    kr = krot[i % 2]
                    v.tensor_scalar_mul(kr[0:64, :], ks[64:128, :], -1.0)
                    v.tensor_copy(kr[64:128, :], ks[0:64, :])
                    v.tensor_tensor(kt1[i % 2], ks, cosv[:, pos0:pos0 + 512],
                                    ALU.mult)
                    v.tensor_tensor(kt2[i % 2], kr, sinv[:, pos0:pos0 + 512],
                                    ALU.mult)
                    v.tensor_tensor(kbf[i % 4], kt1[i % 2], kt2[i % 2], ALU.add
                                    ).then_inc(vc, 1)
                env.mark(f"p1.v.{i}", "vc")
                env.mark(f"p1.free.{i}", "vc")
                if i % n_q1 == n_kq1 - 1:
                    env.mark(f"p1.tabfree.{i // n_q1}", "vc")
                if i >= 2 and kq_index[i - 2] is not None:
                    kqp = kq_index[i - 2]
                    scol = kq_slot[i - 2]
                    env.wait("v", f"p1.nrm.{kqp}")
                    v.reduce_max(nstrip[0:1, scol:scol + 1],
                                 pb[4 + kqp % 2][0:1, 0:512], AX.X
                                 ).then_inc(vc, 1)
                    env.mark(f"p1.nred.{kqp}", "vc")
            for h_ in range(H):
                v.reduce_max(nrmmax[0:1, h_:h_ + 1],
                             nstrip[0:1, h_ * 12:h_ * 12 + 4], AX.X)
                ins = v.reduce_max(nrmmax[0:1, 16 + h_:16 + h_ + 1],
                                   nstrip[0:1, h_ * 12 + 4:h_ * 12 + 12], AX.X)
            ins.then_inc(vc, 1)
            env.mark("p2.nrm2", "vc")
            env.wait("v", "p2.bndsq")
            v.tensor_tensor(bnd_row[0:1, 0:16], bndq_row[0:1, 0:16],
                            bndk_row[0:1, 0:16], ALU.mult).then_inc(vc, 1)
            env.mark("p2.bndneg", "vc")
            env.wait("v", "p2.bndmm")
            v.tensor_copy(negB_sb[:, 0:16], pb[6][:, 0:16]).then_inc(vc, 1)
            env.mark("p2.negB", "vc")
            for gi, (h_, u, c_, nch) in enumerate(p2b_chunks):
                i2 = h_ * NG + u
                band = (c_ >= nch - 8)
                if band:
                    cr = c_ - (nch - 8)
                    if gi == 0:
                        env.wait("v", "p2.mask")
                    env.wait("v", f"p2b.exps.{gi}")
                    if gi >= 4:
                        env.wait("v", f"p2b.pv.{gi - 4}")
                    v.tensor_tensor(ptm[gi % 4], ptb[gi % 4],
                                    mask_sb[:, cr * 512:(cr + 1) * 512],
                                    ALU.mult).then_inc(vc, 1)
                    env.mark(f"p2b.pt.{gi}", "vc")
                if c_ == nch - 1:
                    env.wait("v", f"p2b.pv.{gi}")
                    v.reciprocal(recipS[i2 % 2], pb[5 + i2 % 2]).then_inc(vc, 1)
                    env.mark(f"p2b.recip.{i2}", "vc")
                    if i2 >= 2:
                        env.wait("v", f"p2b.ytspill.{i2 - 2}")
                    else:
                        env.wait("v", "p2.mask")
                    v.tensor_tensor(ytout[i2 % 2], pb[3 + i2 % 2],
                                    recipS[i2 % 2], ALU.mult).then_inc(vc, 1)
                    env.mark(f"p2b.yt.{i2}", "vc")
                    if gi == NCH2B - 1:
                        env.mark("p2.alldone", "vc")
            for g_ in range(NG):
                for cc_ in range(CC):
                    i = g_ * CC + cc_
                    env.wait("v", f"p3.t1a.{i}")
                    env.wait("v", f"p3.xq.{i}")
                    if i >= CC:
                        env.wait("v", f"p3.st1.{i - CC}")
                    v.tensor_tensor(t1sb[:, cc_ * 512:(cc_ + 1) * 512],
                                    t1a[i % 2], xq[i % 3], ALU.add
                                    ).then_inc(vc, 1)
                    env.mark(f"p3.t1.{i}", "vc")
                    env.mark(f"p3.xqfree.{i}", "vc")
                env.wait("v", f"p3.st2.{g_ * CC + CC - 1}")
                v.tensor_scalar_mul(st3[0:1, 0:512], pb[2][0:1, 0:512],
                                    1.0 / 2048)
                v.tensor_scalar_mul(st3[0:1, 512:1024], pb[3][0:1, 0:512],
                                    1.0 / 2048)
                v.tensor_tensor(st3[0:1, 1024:1536], st3[0:1, 0:512],
                                st3[0:1, 0:512], ALU.mult)
                v.tensor_tensor(st3[0:1, 1536:2048], st3[0:1, 512:1024],
                                st3[0:1, 1024:1536], ALU.subtract)
                v.tensor_scalar_add(st3[0:1, 1536:2048], st3[0:1, 1536:2048],
                                    1e-5).then_inc(vc, 1)
                env.mark(f"p3.var.{g_}", "vc")
                env.wait("v", f"p3.rstd.{g_}")
                v.tensor_tensor(st3[0:1, 2560:3072], st3[0:1, 0:512],
                                a_sb[0:1, 0:512], ALU.mult)
                v.tensor_scalar_mul(b_sb[0:1, 0:512], st3[0:1, 2560:3072],
                                    -1.0).then_inc(vc, 1)
                env.mark(f"p3.ab.{g_}", "vc")
                env.wait("v", f"p3.abbc.{g_}")
                v.tensor_copy(a_sb, pb[4])
                v.tensor_copy(b_sb, pb[5]).then_inc(vc, 1)
                env.mark(f"p3.absb.{g_}", "vc")
                for cc_ in range(CC):
                    i = g_ * CC + cc_
                    if i >= 2:
                        env.wait("v", f"p3.lnfout.{i - 2}")
                    v.tensor_tensor(w1b[i % 2],
                                    t1sb[:, cc_ * 512:(cc_ + 1) * 512],
                                    a_sb, ALU.mult)
                    v.tensor_tensor(lnf[i % 2], w1b[i % 2], b_sb, ALU.add
                                    ).then_inc(vc, 1)
                    env.mark(f"p3.lnf.{i}", "vc")
                    if i == NG * CC - 1:
                        env.mark("p3.alldone", "vc")
            for g_ in range(NG):
                for qd in range(16):
                    for s_ in range(4):
                        i = g_ * 64 + qd * 4 + s_
                        env.wait("v", f"p4.sg.{i}")
                        env.wait("v", f"p4.umm.{i}")
                        if g_ >= 1 and qd == 0 and s_ == 0:
                            env.wait("v", f"p4.dmm.{g_ * CC - 1}")
                        v.tensor_tensor(hq[s_], sg[i % 2], pb[2 + s_ % 2],
                                        ALU.mult).then_inc(vc, 1)
                        env.mark(f"p4.hqv.{i}", "vc")
                        env.mark(f"p4.sgfree.{i}", "vc")
                    env.wait("v", f"p4.habs.{g_}.{qd}")
                    a0, a1, a2, a3 = habs
                    v.tensor_tensor(dd[0], a1, a0, ALU.is_gt)
                    v.tensor_tensor(dd[1], a2, a0, ALU.is_gt)
                    v.tensor_tensor(dd[2], a3, a0, ALU.is_gt)
                    v.tensor_tensor(dd[3], a2, a1, ALU.is_gt)
                    v.tensor_tensor(dd[4], a3, a1, ALU.is_gt)
                    v.tensor_tensor(dd[5], a3, a2, ALU.is_gt)
                    v.tensor_tensor(ee[0], dd[0], dd[1], ALU.add)
                    v.tensor_tensor(ee[0], ee[0], dd[2], ALU.add)
                    v.tensor_single_scalar(kkm[0], ee[0], 1.5, ALU.is_le)
                    v.tensor_tensor(ee[1], dd[3], dd[4], ALU.add)
                    v.tensor_tensor(ee[1], ee[1], dd[0], ALU.subtract)
                    v.tensor_single_scalar(kkm[1], ee[1], 0.5, ALU.is_le)
                    v.tensor_tensor(ee[0], dd[1], dd[3], ALU.add)
                    v.tensor_tensor(ee[0], dd[5], ee[0], ALU.subtract)
                    v.tensor_single_scalar(kkm[2], ee[0], -0.5, ALU.is_le)
                    v.tensor_tensor(ee[1], dd[2], dd[4], ALU.add)
                    v.tensor_tensor(ee[1], ee[1], dd[5], ALU.add)
                    v.tensor_single_scalar(kkm[3], ee[1], 1.5, ALU.is_ge)
                    for s_ in range(4):
                        ig = s_ * 16 + qd
                        ins = v.tensor_tensor(hsp[:, ig * 512:(ig + 1) * 512],
                                              hq[s_], kkm[s_], ALU.mult)
                    ins.then_inc(vc, 1)
                    env.mark(f"p4.spq.{g_}.{qd}", "vc")
                env.mark(f"p4.hsp.{g_}", "vc")
                for cc_ in range(CC):
                    i = g_ * CC + cc_
                    env.wait("v", f"p4.dmm.{i}")
                    env.wait("v", f"p4.lnfb.{i}")
                    if i >= CC:
                        env.wait("v", f"p4.st1.{i - CC}")
                    v.tensor_tensor(t2sb[:, cc_ * 512:(cc_ + 1) * 512],
                                    pb[4 + i % 2], lnfb[i % 3], ALU.add
                                    ).then_inc(vc, 1)
                    env.mark(f"p4.t2v.{i}", "vc")
                    env.mark(f"p4.lnffree.{i}", "vc")
                env.wait("v", f"p4.st2.{g_ * CC + CC - 1}")
                v.tensor_scalar_mul(st4[0:1, 0:512], pb[6][0:1, 0:512],
                                    1.0 / 2048)
                v.tensor_scalar_mul(st4[0:1, 512:1024], pb[0][0:1, 0:512],
                                    1.0 / 2048)
                v.tensor_tensor(st4[0:1, 1024:1536], st4[0:1, 0:512],
                                st4[0:1, 0:512], ALU.mult)
                v.tensor_tensor(st4[0:1, 1536:2048], st4[0:1, 512:1024],
                                st4[0:1, 1024:1536], ALU.subtract
                                ).then_inc(vc, 1)
                env.mark(f"p4.var.{g_}", "vc")
                env.wait("v", f"p4.rstd.{g_}")
                v.tensor_tensor(st4[0:1, 2560:3072], st4[0:1, 0:512],
                                a4_sb[0:1, 0:512], ALU.mult)
                v.tensor_scalar_mul(b4_sb[0:1, 0:512], st4[0:1, 2560:3072],
                                    -1.0).then_inc(vc, 1)
                env.mark(f"p4.ab.{g_}", "vc")
                env.wait("v", f"p4.abbc.{g_}")
                v.tensor_copy(a4_sb, pb[0])
                v.tensor_copy(b4_sb, pb[1]).then_inc(vc, 1)
                env.mark(f"p4.absb.{g_}", "vc")
                for cc_ in range(CC):
                    i = g_ * CC + cc_
                    if i >= 2:
                        env.wait("v", f"p4.o2out.{i - 2}")
                    v.tensor_tensor(sq4[i % 2],
                                    t2sb[:, cc_ * 512:(cc_ + 1) * 512],
                                    a4_sb, ALU.mult)
                    v.tensor_tensor(o2[i % 2], sq4[i % 2], b4_sb, ALU.add
                                    ).then_inc(vc, 1)
                    env.mark(f"p4.o2.{i}", "vc")
                env.mark(f"p4.gvec.{g_}.done", "vc")

        mocks = {k: _MockEngine() for k in ("g", "sy", "t", "s", "v")}
        env.mock = True
        env.reset()
        emit_gpsimd(mocks["g"])
        emit_sync(mocks["sy"])
        emit_tensor(mocks["t"])
        emit_scalar(mocks["s"])
        emit_vector(mocks["v"])

        for s in env.sems.values():
            s.h = ctx.enter_context(nc.semaphore(s.name))
        env.mock = False
        env.reset()
        with nc.Block() as block:
            @block.gpsimd
            def _(g):
                emit_gpsimd(_RealEngine(g))

            @block.sync
            def _(sy):
                emit_sync(_RealEngine(sy))

            @block.tensor
            def _(t):
                emit_tensor(_RealEngine(t))

            @block.scalar
            def _(s):
                emit_scalar(_RealEngine(s))

            @block.vector
            def _(v):
                emit_vector(_RealEngine(v))

    return nc



def kernel(**inputs):
    in_maps, own_idx_all = _host_prep(**inputs)
    nc = build_program()
    res = run_bass_kernel_spmd(nc, in_maps, core_ids=list(range(NCORE)))
    out = np.empty((B, T, C), np.float32)
    for core in range(NCORE):
        b = core // 2
        co = np.asarray(res.results[core]["OUT"])
        rows = co.transpose(2, 0, 1).reshape(TOK, C)
        out[b, own_idx_all[core], :] = rows
    return out


# version 13 (final)
# speedup vs baseline: 1.1825x; 1.1825x over previous
import contextlib
import numpy as np
import ml_dtypes
import concourse.bass as bass
import concourse.mybir as mybir
from concourse.bass_utils import run_bass_kernel_spmd

F32 = mybir.dt.float32
BF16 = mybir.dt.bfloat16
AF = mybir.ActivationFunctionType
ALU = mybir.AluOpType
AX = mybir.AxisListType

B, T, C, H, HD, FF = 4, 4096, 2048, 16, 128, 8192
NCORE = 8
TOK = 2048
CC = 16
NG = 4
BFD = ml_dtypes.bfloat16



class _Sem:
    def __init__(self, name):
        self.name = name
        self.n = 0
        self.h = None


class _MockInstr:
    __slots__ = ()
    def then_inc(self, sem, k=1):
        sem.n += k
        return self

_MOCK_INSTR = _MockInstr()


class _MockEngine:
    def wait_ge(self, sem, v):
        pass
    def __getattr__(self, name):
        return lambda *a, **k: _MOCK_INSTR


class _RealInstr:
    __slots__ = ("ins",)
    def __init__(self, ins):
        self.ins = ins
    def then_inc(self, sem, k=1):
        sem.n += k
        self.ins.then_inc(sem.h, k)
        return self


class _RealEngine:
    def __init__(self, eng):
        self._eng = eng
    def wait_ge(self, sem, v):
        self._eng.wait_ge(sem.h, v)
    def __getattr__(self, name):
        eng = self._eng
        def op(*a, **k):
            return _RealInstr(getattr(eng, name)(*a, **k))
        return op


class Env:
    def __init__(self):
        self.sems = {n: _Sem(n) for n in ("mm", "sc", "vc")}
        self.marks = {}
        self.mock = True
        self.eng = {}

    def S(self, name):
        if name not in self.sems:
            assert self.mock, f"new sem {name} in real pass"
            self.sems[name] = _Sem(name)
        return self.sems[name]

    def reset(self):
        for s in self.sems.values():
            s.n = 0

    def mark(self, name, semname):
        s = self.sems[semname]
        if self.mock:
            self.marks[name] = (semname, s.n)
        else:
            assert self.marks[name] == (semname, s.n), (
                name, self.marks[name], (semname, s.n))

    def wait(self, engname, markname):
        if self.mock:
            return
        semname, v = self.marks[markname]
        self.eng[engname].wait_ge(self.sems[semname], v)



def _rope_tables():
    p = np.arange(HD // 2, dtype=np.float32)
    theta = np.power(np.float32(10000.0),
                     (-(np.float32(2.0) ** p) / np.float32(HD)).astype(np.float32),
                     dtype=np.float32)
    pos = np.arange(1, T + 1, dtype=np.float32)
    th2 = np.concatenate([theta, theta]).astype(np.float32)
    ang = (pos[:, None] * th2[None, :]).astype(np.float32)
    return (np.ascontiguousarray(np.cos(ang).T),
            np.ascontiguousarray(np.sin(ang).T))


def _host_prep(L0_attn_x, L0_attn_Wqkv, L0_attn_bqkv, L0_attn_Wo, L0_attn_bo,
               L0_ffn_w_gate, L0_ffn_w_up, L0_ffn_w_down):
    f32 = np.float32
    x = np.asarray(L0_attn_x, f32)
    Wqkv = np.asarray(L0_attn_Wqkv, f32)
    bqkv = np.asarray(L0_attn_bqkv, f32)
    Wo = np.asarray(L0_attn_Wo, f32)
    bo = np.asarray(L0_attn_bo, f32)
    wg = np.asarray(L0_ffn_w_gate, f32)
    wu = np.asarray(L0_ffn_w_up, f32)
    wd = np.asarray(L0_ffn_w_down, f32)

    perm = np.concatenate([np.arange(s, FF, 4) for s in range(4)])

    wqkvT = np.ascontiguousarray(Wqkv.T)
    wqk = np.ascontiguousarray(
        wqkvT[:, :2 * C].reshape(CC, 128, 32, 128).transpose(2, 1, 0, 3)
        .reshape(32, 128, CC * 128))
    wv = np.ascontiguousarray(
        wqkvT[:, 2 * C:].reshape(CC, 128, 4, 512).transpose(2, 1, 0, 3)
        .reshape(4, 128, CC * 512))
    wo_t = np.ascontiguousarray(
        Wo.T.reshape(CC, 128, CC, 128).transpose(2, 1, 0, 3)
        .reshape(CC, 128, CC * 128))
    wg_t = np.ascontiguousarray(
        wg[:, perm].reshape(CC, 128, 64, 128).transpose(2, 1, 0, 3)
        .reshape(64, 128, CC * 128))
    wu_t = np.ascontiguousarray(
        wu[:, perm].reshape(CC, 128, 64, 128).transpose(2, 1, 0, 3)
        .reshape(64, 128, CC * 128))
    wd_t = np.ascontiguousarray(
        wd[perm, :].reshape(64, 128, CC, 128).transpose(2, 1, 0, 3)
        .reshape(CC, 128, 64 * 128))

    bqk_cols = np.ascontiguousarray(bqkv[:2 * C].reshape(32, 128).T)
    bv_rep = np.ascontiguousarray(np.broadcast_to(bqkv[2 * C:], (128, C)))
    bo_cols = np.ascontiguousarray(bo.reshape(CC, 128).T)

    cosT, sinT = _rope_tables()
    ident = np.eye(128, dtype=f32)
    ones1 = np.ones((1, 128), f32)
    onesk = np.ones((128, 1), f32)
    onesm = np.ones((128, 128), f32)
    minus1 = -np.ones((1, 128), f32)

    shared = dict(WQK=wqk, WV=wv, WO=wo_t, WG=wg_t, WU=wu_t, WD=wd_t,
                  BQK=bqk_cols, BV=bv_rep, BO=bo_cols,
                  COS=cosT, SIN=sinT,
                  IDN=ident, ONES1=ones1, ONESK=onesk, ONESM=onesm,
                  MINUS1=minus1)

    in_maps = []
    own_idx_all = []
    for core in range(NCORE):
        b, j = core // 2, core % 2
        starts = np.arange(16) * 256 + 128 * j
        own_idx = (starts[:, None] + np.arange(128)).ravel()
        own_idx_all.append(own_idx)
        xTb = np.ascontiguousarray(x[b].T)
        xTq = np.ascontiguousarray(x[b].T[:, own_idx])
        scale = np.float32(1.0) / np.sqrt(np.float32(HD))
        cosq = np.ascontiguousarray(cosT[:, own_idx] * scale)
        sinq = np.ascontiguousarray(sinT[:, own_idx] * scale)
        q_rel = np.arange(512)
        posq = 256 * (q_rel // 128) + 128 * j + (q_rel % 128)
        kv_rel = np.arange(1024)
        m = (posq[None, :] >= kv_rel[:, None])
        maskPT = np.ascontiguousarray(
            m.reshape(8, 128, 512).transpose(1, 0, 2)
            .reshape(128, 8 * 512).astype(BFD))
        in_maps.append(dict(shared, XT=xTb, XTQ=xTq, COSQ=cosq, SINQ=sinq,
                            MASK=maskPT))
    return in_maps, own_idx_all



class Arena:
    def __init__(self, ap, nbytes):
        self.ap = ap
        self.nbytes = nbytes
        self.off = 0
        self.hi = 0
        self.stack = []

    def push(self):
        self.stack.append(self.off)

    def pop(self):
        self.hi = max(self.hi, self.off)
        self.off = self.stack.pop()

    def carve(self, dtype, cols, rows=128):
        size = 2 if dtype == BF16 else 4
        nb = cols * size
        off = (self.off + 63) & ~63
        self.off = off + nb
        assert self.off <= self.nbytes, (
            f"arena overflow: {self.off} > {self.nbytes}")
        v = self.ap[:, off:off + nb].bitcast(dtype)
        if rows != 128:
            v = v[0:rows, :]
        return v



def build_program(debug_scratch=False):
    nc = bass.Bass("TRN2", target_bir_lowering=False, debug=False)

    D = {}
    def din(name, shape, dtype=F32):
        D[name] = nc.dram_tensor(name, shape, dtype, kind="ExternalInput").ap()
    din("XT", [C, T]); din("XTQ", [C, TOK])
    din("WQK", [32, 128, 2048]); din("WV", [4, 128, 8192])
    din("WO", [CC, 128, 2048])
    din("WG", [64, 128, 2048]); din("WU", [64, 128, 2048])
    din("WD", [CC, 128, 8192])
    din("BQK", [128, 32]); din("BV", [128, C]); din("BO", [128, CC])
    din("COS", [128, T]); din("SIN", [128, T])
    din("COSQ", [128, TOK]); din("SINQ", [128, TOK])
    din("MASK", [128, 8 * 512], BF16)
    din("IDN", [128, 128]); din("ONES1", [1, 128]); din("ONESK", [128, 1])
    din("ONESM", [128, 128]); din("MINUS1", [1, 128])

    OUT = nc.dram_tensor("OUT", [CC, 128, TOK], F32, kind="ExternalOutput").ap()

    skind = "ExternalOutput" if debug_scratch else "Internal"
    QT_s = nc.dram_tensor("QT_s", [H, 128, TOK], BF16, kind=skind).ap()
    KT_s = nc.dram_tensor("KT_s", [H, 128, T], BF16, kind=skind).ap()
    V_s = nc.dram_tensor("V_s", [H, 128, 4096], BF16, kind=skind).ap()
    YT_s = nc.dram_tensor("YT_s", [H * NG, 128, 512], BF16, kind=skind).ap()
    LN1_s = nc.dram_tensor("LN1_s", [CC, 128, TOK], F32, kind=skind).ap()
    LNB_s = nc.dram_tensor("LNB_s", [CC, 128, TOK], BF16, kind="Internal").ap()
    WGB_s = nc.dram_tensor("WGB_s", [64, 128, 2048], BF16, kind="Internal").ap()
    WUB_s = nc.dram_tensor("WUB_s", [64, 128, 2048], BF16, kind="Internal").ap()
    XTB_s = nc.dram_tensor("XTB_s", [CC, 128, 2048], BF16, kind="Internal").ap()
    XQB_s = nc.dram_tensor("XQB_s", [CC, 128, 1024], BF16, kind="Internal").ap()
    DBGNB = (nc.dram_tensor("DBGNB", [128, 16], F32, kind=skind).ap()
             if debug_scratch else None)
    DBGNM = (nc.dram_tensor("DBGNM", [1, 32], F32, kind=skind).ap()
             if debug_scratch else None)

    env = Env()
    ctx = contextlib.ExitStack()
    with ctx:
        ARENA_BYTES = 205 * 1024
        arena_t = ctx.enter_context(
            nc.sbuf_tensor("arena", [128, ARENA_BYTES], mybir.dt.uint8))
        A = Arena(arena_t[:, :], ARENA_BYTES)

        pb = [ctx.enter_context(nc.psum_tensor(f"pb{i}", [128, 512], F32))[:, :]
              for i in range(8)]

        mm, sc, vc = (env.sems[n] for n in ("mm", "sc", "vc"))

        ident_bf = A.carve(BF16, 128)
        ones1_bf = A.carve(BF16, 128, rows=1)
        onesk_bf = A.carve(BF16, 1)
        ones1_f = A.carve(F32, 128, rows=1)
        onesk_f = A.carve(F32, 1)
        onesm_bf = A.carve(BF16, 128)
        minus1_f = A.carve(F32, 128, rows=1)
        bqk_sb = A.carve(F32, 32)
        bo_sb = A.carve(F32, CC)
        nstrip = A.carve(F32, 16 * 12, rows=1)
        nrmmax = A.carve(F32, 32, rows=1)
        bndq_row = A.carve(F32, 16, rows=1)
        bndk_row = A.carve(F32, 16, rows=1)
        bnd_row = A.carve(F32, 16, rows=1)
        negB_sb = A.carve(F32, 16)

        A.push()
        xt = A.carve(BF16, CC * 2048)
        xtq_bf = A.carve(BF16, CC * 1024)
        cos_sb = A.carve(F32, 2048)
        sin_sb = A.carve(F32, 2048)
        cosq_sb = A.carve(F32, 1024)
        sinq_sb = A.carve(F32, 1024)
        bv_sb = A.carve(BF16, C)
        wslab = [A.carve(BF16, 2048) for _ in range(4)]
        wvsl = [A.carve(BF16, CC * 512) for _ in range(2)]
        kstage = [A.carve(F32, 512) for _ in range(4)]
        krot = [A.carve(F32, 512) for _ in range(2)]
        kt1 = [A.carve(F32, 512) for _ in range(2)]
        kt2 = [A.carve(F32, 512) for _ in range(2)]
        kbf = [A.carve(BF16, 512) for _ in range(4)]
        ksq = [A.carve(BF16, 512) for _ in range(2)]
        A.pop()

        A.push()
        mask_sb = A.carve(BF16, 8 * 512)
        qbuf = [A.carve(BF16, TOK) for _ in range(2)]
        kbuf = [A.carve(BF16, T) for _ in range(2)]
        vbuf = [A.carve(BF16, T) for _ in range(2)]
        ptb = [A.carve(BF16, 512) for _ in range(4)]
        ptm = [A.carve(BF16, 512) for _ in range(4)]
        recipS = [A.carve(F32, 512) for _ in range(2)]
        ytout = [A.carve(BF16, 512) for _ in range(2)]
        A.pop()

        A.push()
        ybuf = [A.carve(BF16, H * 512) for _ in range(2)]
        wos = [A.carve(BF16, 2048) for _ in range(2)]
        t1sb = A.carve(F32, CC * 512)
        t1a = [A.carve(F32, 512) for _ in range(2)]
        xq = [A.carve(F32, 512) for _ in range(3)]
        sqb = [A.carve(F32, 512) for _ in range(2)]
        st3 = A.carve(F32, 6 * 512, rows=1)
        a_sb = A.carve(F32, 512)
        b_sb = A.carve(F32, 512)
        w1b = [A.carve(F32, 512) for _ in range(2)]
        lnf = [A.carve(F32, 512) for _ in range(2)]
        lnb = [A.carve(BF16, 512) for _ in range(2)]
        A.pop()

        A.push()
        lnt = A.carve(BF16, CC * 512)
        hsp = A.carve(BF16, 64 * 512)
        wgb = [A.carve(BF16, 2048) for _ in range(2)]
        wub = [A.carve(BF16, 2048) for _ in range(2)]
        wdb = [A.carve(BF16, 4096) for _ in range(2)]
        hq = [A.carve(BF16, 512) for _ in range(4)]
        habs = [A.carve(BF16, 512) for _ in range(4)]
        dd = [A.carve(BF16, 512) for _ in range(6)]
        ee = [A.carve(BF16, 512) for _ in range(2)]
        kkm = [A.carve(BF16, 512) for _ in range(4)]
        sg = [A.carve(F32, 512) for _ in range(2)]
        t2sb = A.carve(F32, CC * 512)
        lnfb = [A.carve(F32, 512) for _ in range(3)]
        sq4 = [A.carve(F32, 512) for _ in range(2)]
        st4 = A.carve(F32, 6 * 512, rows=1)
        a4_sb = A.carve(F32, 512)
        b4_sb = A.carve(F32, 512)
        o2 = [A.carve(F32, 512) for _ in range(2)]
        A.pop()

        def p1_tiles(hx):
            tiles = []
            for j in range(16):
                for pg in (4 * hx, 4 * hx + 1, 4 * hx + 2, 4 * hx + 3):
                    tiles.append(("K", 16 + j, pg))
                for qt in (2 * hx, 2 * hx + 1):
                    tiles.append(("Q", j, qt))
            for dgi in range(4):
                for pc in range(16 * hx, 16 * hx + 16):
                    tiles.append(("V", dgi, pc))
            return tiles

        n_q1 = len(p1_tiles(0))
        g_tiles = sum((p1_tiles(hx) for hx in range(2)), [])
        NT1 = len(g_tiles)
        n_kq1 = 96

        def p1_qx(i):
            return i // n_q1

        def p1_slab_of(tile, hx):
            kind, a_, b_ = tile
            if kind == "V":
                return ("wv", a_, hx)
            return ("wqk", a_, hx)

        slab_seq = []
        for i, tile in enumerate(g_tiles):
            s = p1_slab_of(tile, p1_qx(i))
            if not slab_seq or slab_seq[-1] != s:
                slab_seq.append(s)
        slab_pos = {s: i for i, s in enumerate(slab_seq)}
        NSLAB = len(slab_seq)
        slab_kidx = {}
        _nk = {"wqk": 0, "wv": 0}
        for _si, _s in enumerate(slab_seq):
            slab_kidx[_si] = _nk[_s[0]]
            _nk[_s[0]] += 1
        kindidx_si = {(_s[0], slab_kidx[_si]): _si
                      for _si, _s in enumerate(slab_seq)}

        p2b_chunks = []
        for h_ in range(H):
            for u in range(NG):
                nch = 8 * (u + 1)
                for c_ in range(nch):
                    p2b_chunks.append((h_, u, c_, nch))
        NCH2B = len(p2b_chunks)

        kq_index = {}
        kq_slot = {}
        _kq = 0
        for _i, (_kind, _a, _b) in enumerate(g_tiles):
            if _kind in ("K", "Q"):
                kq_index[_i] = _kq
                if _kind == "Q":
                    kq_slot[_i] = _a * 12 + _b
                else:
                    kq_slot[_i] = (_a - 16) * 12 + 4 + _b
                _kq += 1
            else:
                kq_index[_i] = None
        NKQ = _kq


        def emit_gpsimd(g):
            env.eng["g"] = g
            for name, dst in (("IDN", ident_bf), ("ONES1", ones1_bf),
                              ("ONESK", onesk_bf), ("ONESM", onesm_bf),
                              ("BV", bv_sb)):
                g.dma_start(out=dst, in_=D[name]).then_inc(env.S("gcbf"), 16)
            env.mark("consts_bf", "gcbf")
            sx = env.S("gxt")
            for cc_ in range(CC):
                g.dma_start(out=xt[:, cc_ * 2048:(cc_ + 1) * 2048],
                            in_=D["XT"][cc_ * 128:(cc_ + 1) * 128, 0:2048]
                            ).then_inc(sx, 16)
            for cc_ in range(CC):
                g.dma_start(out=xtq_bf[:, cc_ * 1024:(cc_ + 1) * 1024],
                            in_=D["XTQ"][cc_ * 128:(cc_ + 1) * 128, 0:1024]
                            ).then_inc(sx, 16)
            env.mark("p1.xt.0", "gxt")
            for si, (kind, idx, hx) in enumerate(slab_seq):
                if si < CC:
                    gb = env.S("gxtb")
                    g.dma_start(out=XTB_s[si],
                                in_=D["XT"][si * 128:(si + 1) * 128, 2048:4096]
                                ).then_inc(gb, 16)
                    g.dma_start(out=XQB_s[si],
                                in_=D["XTQ"][si * 128:(si + 1) * 128, 1024:2048]
                                ).then_inc(gb, 16)
                    if si == CC - 1:
                        env.mark("p0.xtb", "gxtb")
                kidx = slab_kidx[si]
                depth = 4 if kind == "wqk" else 2
                if kidx >= depth:
                    env.wait("g",
                             f"p1.slabfree.{kindidx_si[(kind, kidx - depth)]}")
                src = D["WQK"][idx] if kind == "wqk" else D["WV"][idx]
                buf = wslab[kidx % 4] if kind == "wqk" else wvsl[kidx % 2]
                sname = (f"gslq{kidx % 4}" if kind == "wqk"
                         else f"gslv{kidx % 2}")
                g.dma_start(out=buf, in_=src).then_inc(env.S(sname), 16)
                env.mark(f"p1.slab.{si}", sname)
            for qd in range(16):
                for s_ in range(4):
                    ig = s_ * 16 + qd
                    j = qd * 4 + s_
                    gcv = env.S("gcv")
                    g.dma_start(out=WGB_s[ig], in_=D["WG"][ig]).then_inc(gcv, 16)
                    g.dma_start(out=WUB_s[ig], in_=D["WU"][ig]).then_inc(gcv, 16)
                    env.mark(f"p0.cv.{j}", "gcv")
            for g_ in range(NG):
                for cc_ in range(CC):
                    i = g_ * CC + cc_
                    if i == 0:
                        env.wait("g", "p2.alldone")
                    if i >= 2:
                        env.wait("g", f"p3.slabfree.{i - 2}")
                    g.dma_start(out=wos[i % 2], in_=D["WO"][cc_]
                                ).then_inc(env.S(f"gwo{i % 2}"), 16)
                    env.mark(f"p3.slab.{i}", f"gwo{i % 2}")
            for g_ in range(NG):
                for cc_ in range(CC):
                    for hh in range(2):
                        i = (g_ * CC + cc_) * 2 + hh
                        if i == 0:
                            env.wait("g", "p3.alldone")
                        if i >= 2:
                            env.wait("g", f"p4.wdfree.{i - 2}")
                        g.dma_start(out=wdb[i % 2],
                                    in_=D["WD"][cc_][:, hh * 4096:(hh + 1) * 4096]
                                    ).then_inc(env.S(f"gwd{i % 2}"), 16)
                        env.mark(f"p4.wd.{i}", f"gwd{i % 2}")

        def emit_sync(sy):
            env.eng["sy"] = sy
            yc = env.S("ycst")
            sy.dma_start(out=ones1_f, in_=D["ONES1"]).then_inc(yc, 16)
            sy.dma_start(out=onesk_f, in_=D["ONESK"]).then_inc(yc, 16)
            sy.dma_start(out=minus1_f, in_=D["MINUS1"]).then_inc(yc, 16)
            sy.dma_start(out=bqk_sb, in_=D["BQK"]).then_inc(yc, 16)
            sy.dma_start(out=bo_sb, in_=D["BO"]).then_inc(yc, 16)
            env.mark("consts", "ycst")
            def tab_load(hn):
                yt_ = env.S("ytab")
                sy.dma_start(out=cos_sb,
                             in_=D["COS"][:, hn * 2048:(hn + 1) * 2048]
                             ).then_inc(yt_, 16)
                sy.dma_start(out=sin_sb,
                             in_=D["SIN"][:, hn * 2048:(hn + 1) * 2048]
                             ).then_inc(yt_, 16)
                sy.dma_start(out=cosq_sb,
                             in_=D["COSQ"][:, hn * 1024:(hn + 1) * 1024]
                             ).then_inc(yt_, 16)
                sy.dma_start(out=sinq_sb,
                             in_=D["SINQ"][:, hn * 1024:(hn + 1) * 1024]
                             ).then_inc(yt_, 16)
                env.mark(f"p1.tab.{hn}", "ytab")
            tab_load(0)
            for i, tile in enumerate(g_tiles):
                if i == n_kq1:
                    env.wait("sy", "p1.tabfree.0")
                    tab_load(1)
                if i == n_q1:
                    env.wait("sy", "p1.quarter.0.done")
                    env.wait("sy", "p0.xtb")
                    yx = env.S("yxt")
                    for cc_ in range(CC):
                        sy.dma_start(out=xt[:, cc_ * 2048:(cc_ + 1) * 2048],
                                     in_=XTB_s[cc_]).then_inc(yx, 16)
                    for cc_ in range(CC):
                        sy.dma_start(out=xtq_bf[:, cc_ * 1024:(cc_ + 1) * 1024],
                                     in_=XQB_s[cc_]).then_inc(yx, 16)
                    env.mark("p1.xts", "yxt")
                kind, a_, b_ = tile
                env.wait("sy", f"p1.v.{i}")
                ys = env.S(f"ysp{i % 8}")
                if kind == "K":
                    sy.dma_start(out=KT_s[a_ - 16][:, b_ * 512:(b_ + 1) * 512],
                                 in_=kbf[i % 4]).then_inc(ys, 16)
                elif kind == "Q":
                    sy.dma_start(out=QT_s[a_][:, b_ * 512:(b_ + 1) * 512],
                                 in_=kbf[i % 4]).then_inc(ys, 16)
                else:
                    for hh in range(4):
                        sy.dma_start(out=V_s[a_ * 4 + hh][:, b_ * 128:(b_ + 1) * 128],
                                     in_=kbf[i % 4][:, hh * 128:(hh + 1) * 128]
                                     ).then_inc(ys, 16)
                env.mark(f"p1.spill.{i}", f"ysp{i % 8}")
            for dd_ in range(1, 9):
                env.wait("sy", f"p1.spill.{NT1 - dd_}")
            env.wait("sy", "p2.negB")
            if debug_scratch:
                sy.dma_start(out=DBGNB, in_=negB_sb[:, 0:16]
                             ).then_inc(env.S("ymsk"), 16)
                sy.dma_start(out=DBGNM, in_=nrmmax[0:1, 0:32]
                             ).then_inc(env.S("ymsk"), 16)
            sy.dma_start(out=mask_sb, in_=D["MASK"]).then_inc(env.S("ymsk"), 16)
            env.mark("p2.mask", "ymsk")
            for h_ in range(H + 1):
                if h_ < H:
                    if h_ >= 2:
                        env.wait("sy", f"p2b.head.{h_ - 2}.done")
                    yl = env.S(f"yld{h_ % 2}")
                    sy.dma_start(out=qbuf[h_ % 2], in_=QT_s[h_]).then_inc(yl, 16)
                    sy.dma_start(out=kbuf[h_ % 2], in_=KT_s[h_]).then_inc(yl, 16)
                    sy.dma_start(out=vbuf[h_ % 2], in_=V_s[h_]).then_inc(yl, 16)
                    env.mark(f"p2b.load.{h_}", f"yld{h_ % 2}")
                if h_ >= 1:
                    for u in range(NG):
                        i2 = (h_ - 1) * NG + u
                        env.wait("sy", f"p2b.yt.{i2}")
                        sy.dma_start(out=YT_s[i2], in_=ytout[i2 % 2]
                                     ).then_inc(env.S(f"yyt{i2 % 4}"), 16)
                        env.mark(f"p2b.ytspill.{i2}", f"yyt{i2 % 4}")
            env.wait("sy", "p2.alldone")
            for g_ in range(NG):
                if g_ >= 2:
                    env.wait("sy", f"p3.g.{g_ - 2}.done")
                env.wait("sy", f"p2b.ytspill.{60 + g_}")
                if 59 + g_ >= 0:
                    env.wait("sy", f"p2b.ytspill.{59 + g_}")
                yb = env.S(f"yyb{g_ % 2}")
                for h_ in range(H):
                    sy.dma_start(out=ybuf[g_ % 2][:, h_ * 512:(h_ + 1) * 512],
                                 in_=YT_s[h_ * NG + g_]).then_inc(yb, 16)
                env.mark(f"p3.ybuf.{g_}", f"yyb{g_ % 2}")
                for cc_ in range(CC):
                    i = g_ * CC + cc_
                    if i >= 3:
                        env.wait("sy", f"p3.xqfree.{i - 3}")
                    sy.dma_start(out=xq[i % 3],
                                 in_=D["XTQ"][cc_ * 128:(cc_ + 1) * 128,
                                              g_ * 512:(g_ + 1) * 512]
                                 ).then_inc(env.S(f"yxq{i % 3}"), 16)
                    env.mark(f"p3.xq.{i}", f"yxq{i % 3}")
                for cc_ in range(CC):
                    i = g_ * CC + cc_
                    env.wait("sy", f"p3.lnf.{i}")
                    sy.dma_start(out=LN1_s[cc_][:, g_ * 512:(g_ + 1) * 512],
                                 in_=lnf[i % 2]).then_inc(env.S(f"ylo{i % 4}"), 16)
                    env.mark(f"p3.lnfout.{i}", f"ylo{i % 4}")
                    env.wait("sy", f"p3.lnb.{i}")
                    sy.dma_start(out=LNB_s[cc_][:, g_ * 512:(g_ + 1) * 512],
                                 in_=lnb[i % 2]).then_inc(env.S(f"ylb{i % 4}"), 16)
                    env.mark(f"p3.lnbout.{i}", f"ylb{i % 4}")
            env.wait("sy", "p3.alldone")
            for g_ in range(NG):
                if g_ >= 1:
                    env.wait("sy", f"p4.gvec.{g_ - 1}.done")
                    env.wait("sy", f"p4.g.{g_ - 1}.done")
                for cc_ in range(CC):
                    sy.dma_start(out=lnt[:, cc_ * 512:(cc_ + 1) * 512],
                                 in_=LNB_s[cc_][:, g_ * 512:(g_ + 1) * 512]
                                 ).then_inc(env.S("ylnt"), 16)
                env.mark(f"p4.lnt.{g_}", "ylnt")
                for qd in range(16):
                    for s_ in range(4):
                        i = g_ * 64 + qd * 4 + s_
                        if i >= 2:
                            env.wait("sy", f"p4.gufree.{i - 2}")
                        ig = s_ * 16 + qd
                        env.wait("sy", f"p0.cv.{qd * 4 + s_}")
                        yg = env.S(f"ygu{i % 2}")
                        sy.dma_start(out=wgb[i % 2], in_=WGB_s[ig]
                                     ).then_inc(yg, 16)
                        env.mark(f"p4.gu.{i}", f"ygu{i % 2}")
                for cc_ in range(CC):
                    i = g_ * CC + cc_
                    if i >= 3:
                        env.wait("sy", f"p4.lnffree.{i - 3}")
                    sy.dma_start(out=lnfb[i % 3],
                                 in_=LN1_s[cc_][:, g_ * 512:(g_ + 1) * 512]
                                 ).then_inc(env.S(f"ylf{i % 3}"), 16)
                    env.mark(f"p4.lnfb.{i}", f"ylf{i % 3}")
                for cc_ in range(CC):
                    i = g_ * CC + cc_
                    env.wait("sy", f"p4.o2.{i}")
                    sy.dma_start(out=OUT[cc_][:, g_ * 512:(g_ + 1) * 512],
                                 in_=o2[i % 2]).then_inc(env.S(f"yo2{i % 4}"), 16)
                    env.mark(f"p4.o2out.{i}", f"yo2{i % 4}")
            for dd_ in range(1, 5):
                env.wait("sy", f"p4.o2out.{NG * CC - dd_}")

        def emit_tensor(t):
            env.eng["t"] = t
            env.wait("t", "consts")
            env.wait("t", "consts_bf")
            for i, tile in enumerate(g_tiles):
                kind, a_, b_ = tile
                qx = p1_qx(i)
                env.wait("t", "p1.xt.0" if qx == 0 else "p1.xts")
                si = slab_pos[p1_slab_of(tile, qx)]
                env.wait("t", f"p1.slab.{si}")
                if i >= 4:
                    env.wait("t", f"p1.free.{i - 4}")
                bank = pb[i % 4]
                if kind == "K":
                    sl = wslab[slab_kidx[si] % 4]
                    pos0 = b_ * 512 - 2048 * qx
                    for cc_ in range(CC):
                        ins = t.matmul(
                            bank, sl[:, cc_ * 128:(cc_ + 1) * 128],
                            xt[:, cc_ * 2048 + pos0:cc_ * 2048 + pos0 + 512],
                            start=(cc_ == 0), stop=(cc_ == CC - 1))
                elif kind == "Q":
                    sl = wslab[slab_kidx[si] % 4]
                    posq = (b_ % 2) * 512
                    for cc_ in range(CC):
                        ins = t.matmul(
                            bank, sl[:, cc_ * 128:(cc_ + 1) * 128],
                            xtq_bf[:, cc_ * 1024 + posq:cc_ * 1024 + posq + 512],
                            start=(cc_ == 0), stop=(cc_ == CC - 1))
                else:
                    sl = wvsl[slab_kidx[si] % 2]
                    pos0 = b_ * 128 - 2048 * qx
                    for cc_ in range(CC):
                        ins = t.matmul(
                            bank,
                            xt[:, cc_ * 2048 + pos0:cc_ * 2048 + pos0 + 128],
                            sl[:, cc_ * 512:(cc_ + 1) * 512],
                            start=(cc_ == 0), stop=(cc_ == CC - 1))
                ins.then_inc(mm, 1)
                env.mark(f"p1.mm.{i}", "mm")
                nxt_si = (slab_pos[p1_slab_of(g_tiles[i + 1], p1_qx(i + 1))]
                          if i + 1 < NT1 else None)
                if nxt_si != si:
                    env.mark(f"p1.slabfree.{si}", "mm")
                if (i + 1) % n_q1 == 0:
                    env.mark(f"p1.quarter.{i // n_q1}.done", "mm")
                if i >= 1 and kq_index[i - 1] is not None:
                    kqp = kq_index[i - 1]
                    env.wait("t", f"p1.ksq.{kqp}")
                    if kqp >= 2:
                        env.wait("t", f"p1.nred.{kqp - 2}")
                    t.matmul(pb[4 + kqp % 2][0:1, :], onesk_bf, ksq[kqp % 2],
                             start=True, stop=True).then_inc(mm, 1)
                    env.mark(f"p1.nrm.{kqp}", "mm")
            env.wait("t", "p2.bndneg")
            t.matmul(pb[6][:, 0:16], minus1_f, bnd_row[0:1, 0:16],
                     start=True, stop=True).then_inc(mm, 1)
            env.mark("p2.bndmm", "mm")

            def p2b_tail(gj):
                hj, uj, cj, nchj = p2b_chunks[gj]
                i2j = hj * NG + uj
                band = (cj >= nchj - 8)
                if band:
                    env.wait("t", f"p2b.pt.{gj}")
                else:
                    env.wait("t", f"p2b.exps.{gj}")
                if cj == 0 and i2j >= 2:
                    env.wait("t", f"p2b.yt.{i2j - 2}")
                    env.wait("t", f"p2b.recip.{i2j - 2}")
                src = ptm[gj % 4] if band else ptb[gj % 4]
                t.matmul(pb[5 + i2j % 2], onesm_bf, src,
                         start=(cj == 0), stop=(cj == nchj - 1))
                t.matmul(pb[3 + i2j % 2],
                         vbuf[hj % 2][:, cj * 128:(cj + 1) * 128],
                         src, start=(cj == 0), stop=(cj == nchj - 1)
                         ).then_inc(mm, 1)
                env.mark(f"p2b.pv.{gj}", "mm")
                if cj == nchj - 1 and uj == NG - 1:
                    env.mark(f"p2b.head.{hj}.done", "mm")

            next_tail = [0]

            def flush_tails(n):
                while next_tail[0] < n:
                    p2b_tail(next_tail[0])
                    next_tail[0] += 1

            srot = (0, 1, 2, 7)
            for gi, (h_, u, c_, nch) in enumerate(p2b_chunks):
                if u == 0 and c_ == 0:
                    flush_tails(gi)
                    env.wait("t", f"p2b.load.{h_}")
                if gi >= 4:
                    env.wait("t", f"p2b.exps.{gi - 4}")
                t.matmul(pb[srot[gi % 4]],
                         kbuf[h_ % 2][:, c_ * 128:(c_ + 1) * 128],
                         qbuf[h_ % 2][:, u * 512:(u + 1) * 512],
                         start=True, stop=True).then_inc(mm, 1)
                env.mark(f"p2b.st.{gi}", "mm")
                flush_tails(gi - 1)
            flush_tails(NCH2B)
            for g_ in range(NG):
                env.wait("t", f"p3.ybuf.{g_}")
                for cc_ in range(CC):
                    i = g_ * CC + cc_
                    env.wait("t", f"p3.slab.{i}")
                    if i >= 2:
                        env.wait("t", f"p3.t1a.{i - 2}")
                    bank = pb[i % 2]
                    for h_ in range(H):
                        ins = t.matmul(
                            bank, wos[i % 2][:, h_ * 128:(h_ + 1) * 128],
                            ybuf[g_ % 2][:, h_ * 512:(h_ + 1) * 512],
                            start=(h_ == 0), stop=(h_ == H - 1))
                    ins.then_inc(mm, 1)
                    env.mark(f"p3.o.{i}", "mm")
                    env.mark(f"p3.slabfree.{i}", "mm")
                    env.wait("t", f"p3.t1.{i}")
                    if cc_ == 0 and g_ >= 1:
                        env.wait("t", f"p3.var.{g_ - 1}")
                    t.matmul(pb[2][0:1, :], onesk_f,
                             t1sb[:, cc_ * 512:(cc_ + 1) * 512],
                             start=(cc_ == 0), stop=(cc_ == CC - 1)
                             ).then_inc(mm, 1)
                    env.mark(f"p3.st1.{i}", "mm")
                    env.wait("t", f"p3.sq.{i}")
                    t.matmul(pb[3][0:1, :], onesk_f, sqb[i % 2],
                             start=(cc_ == 0), stop=(cc_ == CC - 1)
                             ).then_inc(mm, 1)
                    env.mark(f"p3.st2.{i}", "mm")
                env.wait("t", f"p3.ab.{g_}")
                t.matmul(pb[4], ones1_f, a_sb[0:1, :], start=True, stop=True)
                t.matmul(pb[5], ones1_f, b_sb[0:1, :], start=True, stop=True
                         ).then_inc(mm, 1)
                env.mark(f"p3.abbc.{g_}", "mm")
                env.wait("t", f"p3.absb.{g_}")
                env.mark(f"p3.g.{g_}.done", "mm")
            for g_ in range(NG):
                env.wait("t", f"p4.lnt.{g_}")
                for qd in range(16):
                    for s_ in range(4):
                        i = g_ * 64 + qd * 4 + s_
                        env.wait("t", f"p4.gu.{i}")
                        if i >= 2:
                            env.wait("t", f"p4.sg.{i - 2}")
                        gbank = pb[s_ % 2]
                        for cc_ in range(CC):
                            ins = t.matmul(
                                gbank, wgb[i % 2][:, cc_ * 128:(cc_ + 1) * 128],
                                lnt[:, cc_ * 512:(cc_ + 1) * 512],
                                start=(cc_ == 0), stop=(cc_ == CC - 1))
                        ins.then_inc(mm, 1)
                        env.mark(f"p4.gmm.{i}", "mm")
                        env.wait("t", f"p4.wub.{i}")
                        if i >= 2:
                            env.wait("t", f"p4.hqv.{i - 2}")
                        ubank = pb[2 + s_ % 2]
                        for cc_ in range(CC):
                            ins = t.matmul(
                                ubank, wub[i % 2][:, cc_ * 128:(cc_ + 1) * 128],
                                lnt[:, cc_ * 512:(cc_ + 1) * 512],
                                start=(cc_ == 0), stop=(cc_ == CC - 1))
                        ins.then_inc(mm, 1)
                        env.mark(f"p4.umm.{i}", "mm")
                        env.mark(f"p4.gufree.{i}", "mm")
                env.wait("t", f"p4.hsp.{g_}")
                for cc_ in range(CC):
                    i = g_ * CC + cc_
                    if i >= 2:
                        env.wait("t", f"p4.t2v.{i - 2}")
                    bank = pb[4 + i % 2]
                    for hh in range(2):
                        wi = i * 2 + hh
                        env.wait("t", f"p4.wd.{wi}")
                        for q_ in range(32):
                            ig = hh * 32 + q_
                            ins = t.matmul(
                                bank, wdb[wi % 2][:, q_ * 128:(q_ + 1) * 128],
                                hsp[:, ig * 512:(ig + 1) * 512],
                                start=(ig == 0), stop=(ig == 63))
                        ins.then_inc(mm, 1)
                        env.mark(f"p4.wdfree.{wi}", "mm")
                    env.mark(f"p4.dmm.{i}", "mm")
                    env.wait("t", f"p4.t2v.{i}")
                    if cc_ == 0 and g_ >= 1:
                        env.wait("t", f"p4.var.{g_ - 1}")
                    t.matmul(pb[6][0:1, :], onesk_f,
                             t2sb[:, cc_ * 512:(cc_ + 1) * 512],
                             start=(cc_ == 0), stop=(cc_ == CC - 1)
                             ).then_inc(mm, 1)
                    env.mark(f"p4.st1.{i}", "mm")
                    env.wait("t", f"p4.sq.{i}")
                    t.matmul(pb[0][0:1, :], onesk_f, sq4[i % 2],
                             start=(cc_ == 0), stop=(cc_ == CC - 1)
                             ).then_inc(mm, 1)
                    env.mark(f"p4.st2.{i}", "mm")
                env.wait("t", f"p4.ab.{g_}")
                t.matmul(pb[0], ones1_f, a4_sb[0:1, :], start=True, stop=True)
                t.matmul(pb[1], ones1_f, b4_sb[0:1, :], start=True, stop=True
                         ).then_inc(mm, 1)
                env.mark(f"p4.abbc.{g_}", "mm")
                env.wait("t", f"p4.absb.{g_}")
                env.mark(f"p4.g.{g_}.done", "mm")

        def emit_scalar(s):
            env.eng["s"] = s
            env.wait("s", "consts")
            prev_kq = None
            for i, tile in enumerate(g_tiles):
                kind, a_, b_ = tile
                if kind != "V":
                    env.wait("s", f"p1.mm.{i}")
                    if i >= 4:
                        env.wait("s", f"p1.v.{i - 4}")
                    s.add(kstage[i % 4], pb[i % 4], bqk_sb[:, a_:a_ + 1]
                          ).then_inc(sc, 1)
                    env.mark(f"p1.sc.{i}", "sc")
                if prev_kq is not None:
                    kqp = kq_index[prev_kq]
                    env.wait("s", f"p1.v.{prev_kq}")
                    if kqp >= 2:
                        env.wait("s", f"p1.nrm.{kqp - 2}")
                    s.activation(ksq[kqp % 2], kbf[prev_kq % 4], AF.Square
                                 ).then_inc(sc, 1)
                    env.mark(f"p1.ksq.{kqp}", "sc")
                    prev_kq = None
                if kind != "V":
                    prev_kq = i
            env.wait("s", "p2.nrm2")
            s.activation(bndq_row[0:1, 0:16], nrmmax[0:1, 0:16], AF.Sqrt,
                         scale=1.1025)
            s.activation(bndk_row[0:1, 0:16], nrmmax[0:1, 16:32], AF.Sqrt
                         ).then_inc(sc, 1)
            env.mark("p2.bndsq", "sc")
            for gi, (h_, u, c_, nch) in enumerate(p2b_chunks):
                if gi == 0:
                    env.wait("s", "p2.negB")
                env.wait("s", f"p2b.st.{gi}")
                if gi >= 4:
                    env.wait("s", f"p2b.pv.{gi - 4}")
                s.activation(ptb[gi % 4], pb[(0, 1, 2, 7)[gi % 4]], AF.Exp,
                             bias=negB_sb[:, h_:h_ + 1]).then_inc(sc, 1)
                env.mark(f"p2b.exps.{gi}", "sc")
            for g_ in range(NG):
                for cc_ in range(CC):
                    i = g_ * CC + cc_
                    env.wait("s", f"p3.o.{i}")
                    if i >= 2:
                        env.wait("s", f"p3.t1.{i - 2}")
                    s.add(t1a[i % 2], pb[i % 2], bo_sb[:, cc_:cc_ + 1]
                          ).then_inc(sc, 1)
                    env.mark(f"p3.t1a.{i}", "sc")
                    env.wait("s", f"p3.t1.{i}")
                    if i >= 2:
                        env.wait("s", f"p3.st2.{i - 2}")
                    s.activation(sqb[i % 2],
                                 t1sb[:, cc_ * 512:(cc_ + 1) * 512], AF.Square
                                 ).then_inc(sc, 1)
                    env.mark(f"p3.sq.{i}", "sc")
                env.wait("s", f"p3.var.{g_}")
                s.activation(st3[0:1, 2048:2560], st3[0:1, 1536:2048], AF.Ln
                             ).then_inc(sc, 1)
                s.activation(a_sb[0:1, 0:512], st3[0:1, 2048:2560], AF.Exp,
                             scale=-0.5).then_inc(sc, 1)
                env.mark(f"p3.rstd.{g_}", "sc")
                for cc_ in range(CC):
                    i = g_ * CC + cc_
                    env.wait("s", f"p3.lnf.{i}")
                    if i >= 2:
                        env.wait("s", f"p3.lnbout.{i - 2}")
                    s.copy(lnb[i % 2], lnf[i % 2]).then_inc(sc, 1)
                    env.mark(f"p3.lnb.{i}", "sc")
            def wub_load(iw):
                qd_, s2 = (iw % 64) // 4, iw % 4
                if iw >= 2:
                    env.wait("s", f"p4.gufree.{iw - 2}")
                env.wait("s", f"p0.cv.{qd_ * 4 + s2}")
                s.dma_start(out=wub[iw % 2], in_=WUB_s[s2 * 16 + qd_]
                            ).then_inc(env.S(f"sgu{iw % 2}"), 16)
                env.mark(f"p4.wub.{iw}", f"sgu{iw % 2}")
            env.wait("s", "p3.alldone")
            wub_load(0)
            wub_load(1)
            for g_ in range(NG):
                for qd in range(16):
                    for s_ in range(4):
                        i = g_ * 64 + qd * 4 + s_
                        if i + 2 < NG * 64:
                            wub_load(i + 2)
                        env.wait("s", f"p4.gmm.{i}")
                        if i >= 2:
                            env.wait("s", f"p4.sgfree.{i - 2}")
                        s.activation(sg[i % 2], pb[s_ % 2], AF.Silu
                                     ).then_inc(sc, 1)
                        env.mark(f"p4.sg.{i}", "sc")
                    for s_ in range(4):
                        i = g_ * 64 + qd * 4 + s_
                        env.wait("s", f"p4.hqv.{i}")
                        s.activation(habs[s_], hq[s_], AF.Abs).then_inc(sc, 1)
                    env.mark(f"p4.habs.{g_}.{qd}", "sc")
                for cc_ in range(CC):
                    i = g_ * CC + cc_
                    env.wait("s", f"p4.t2v.{i}")
                    if i >= 2:
                        env.wait("s", f"p4.st2.{i - 2}")
                    s.activation(sq4[i % 2],
                                 t2sb[:, cc_ * 512:(cc_ + 1) * 512], AF.Square
                                 ).then_inc(sc, 1)
                    env.mark(f"p4.sq.{i}", "sc")
                env.wait("s", f"p4.var.{g_}")
                s.activation(st4[0:1, 2048:2560], st4[0:1, 1536:2048], AF.Ln
                             ).then_inc(sc, 1)
                s.activation(a4_sb[0:1, 0:512], st4[0:1, 2048:2560], AF.Exp,
                             scale=-0.5).then_inc(sc, 1)
                env.mark(f"p4.rstd.{g_}", "sc")

        def emit_vector(v):
            env.eng["v"] = v
            env.wait("v", "consts")
            env.wait("v", "consts_bf")
            for i, tile in enumerate(g_tiles):
                kind, a_, b_ = tile
                qx = p1_qx(i)
                if i >= 4:
                    env.wait("v", f"p1.spill.{i - 4}")
                    if kq_index[i - 4] is not None:
                        env.wait("v", f"p1.ksq.{kq_index[i - 4]}")
                if kind == "V":
                    env.wait("v", f"p1.mm.{i}")
                    v.tensor_tensor(kbf[i % 4], pb[i % 4],
                                    bv_sb[:, a_ * 512:(a_ + 1) * 512], ALU.add
                                    ).then_inc(vc, 1)
                else:
                    env.wait("v", f"p1.sc.{i}")
                    env.wait("v", f"p1.tab.{qx}")
                    if kind == "K":
                        pos0 = b_ * 512 - 2048 * qx
                        cosv, sinv = cos_sb, sin_sb
                    else:
                        pos0 = (b_ % 2) * 512
                        cosv, sinv = cosq_sb, sinq_sb
                    ks = kstage[i % 4]
                    kr = krot[i % 2]
                    v.tensor_scalar_mul(kr[0:64, :], ks[64:128, :], -1.0)
                    v.tensor_copy(kr[64:128, :], ks[0:64, :])
                    v.tensor_tensor(kt1[i % 2], ks, cosv[:, pos0:pos0 + 512],
                                    ALU.mult)
                    v.tensor_tensor(kt2[i % 2], kr, sinv[:, pos0:pos0 + 512],
                                    ALU.mult)
                    v.tensor_tensor(kbf[i % 4], kt1[i % 2], kt2[i % 2], ALU.add
                                    ).then_inc(vc, 1)
                env.mark(f"p1.v.{i}", "vc")
                env.mark(f"p1.free.{i}", "vc")
                if i % n_q1 == n_kq1 - 1:
                    env.mark(f"p1.tabfree.{i // n_q1}", "vc")
                if i >= 2 and kq_index[i - 2] is not None:
                    kqp = kq_index[i - 2]
                    scol = kq_slot[i - 2]
                    env.wait("v", f"p1.nrm.{kqp}")
                    v.reduce_max(nstrip[0:1, scol:scol + 1],
                                 pb[4 + kqp % 2][0:1, 0:512], AX.X
                                 ).then_inc(vc, 1)
                    env.mark(f"p1.nred.{kqp}", "vc")
            for h_ in range(H):
                v.reduce_max(nrmmax[0:1, h_:h_ + 1],
                             nstrip[0:1, h_ * 12:h_ * 12 + 4], AX.X)
                ins = v.reduce_max(nrmmax[0:1, 16 + h_:16 + h_ + 1],
                                   nstrip[0:1, h_ * 12 + 4:h_ * 12 + 12], AX.X)
            ins.then_inc(vc, 1)
            env.mark("p2.nrm2", "vc")
            env.wait("v", "p2.bndsq")
            v.tensor_tensor(bnd_row[0:1, 0:16], bndq_row[0:1, 0:16],
                            bndk_row[0:1, 0:16], ALU.mult).then_inc(vc, 1)
            env.mark("p2.bndneg", "vc")
            env.wait("v", "p2.bndmm")
            v.tensor_copy(negB_sb[:, 0:16], pb[6][:, 0:16]).then_inc(vc, 1)
            env.mark("p2.negB", "vc")
            for gi, (h_, u, c_, nch) in enumerate(p2b_chunks):
                i2 = h_ * NG + u
                band = (c_ >= nch - 8)
                if band:
                    cr = c_ - (nch - 8)
                    if gi == 0:
                        env.wait("v", "p2.mask")
                    env.wait("v", f"p2b.exps.{gi}")
                    if gi >= 4:
                        env.wait("v", f"p2b.pv.{gi - 4}")
                    v.tensor_tensor(ptm[gi % 4], ptb[gi % 4],
                                    mask_sb[:, cr * 512:(cr + 1) * 512],
                                    ALU.mult).then_inc(vc, 1)
                    env.mark(f"p2b.pt.{gi}", "vc")
                if c_ == nch - 1:
                    env.wait("v", f"p2b.pv.{gi}")
                    v.reciprocal(recipS[i2 % 2], pb[5 + i2 % 2]).then_inc(vc, 1)
                    env.mark(f"p2b.recip.{i2}", "vc")
                    if i2 >= 2:
                        env.wait("v", f"p2b.ytspill.{i2 - 2}")
                    else:
                        env.wait("v", "p2.mask")
                    v.tensor_tensor(ytout[i2 % 2], pb[3 + i2 % 2],
                                    recipS[i2 % 2], ALU.mult).then_inc(vc, 1)
                    env.mark(f"p2b.yt.{i2}", "vc")
                    if gi == NCH2B - 1:
                        env.mark("p2.alldone", "vc")
            for g_ in range(NG):
                for cc_ in range(CC):
                    i = g_ * CC + cc_
                    env.wait("v", f"p3.t1a.{i}")
                    env.wait("v", f"p3.xq.{i}")
                    if i >= CC:
                        env.wait("v", f"p3.st1.{i - CC}")
                    v.tensor_tensor(t1sb[:, cc_ * 512:(cc_ + 1) * 512],
                                    t1a[i % 2], xq[i % 3], ALU.add
                                    ).then_inc(vc, 1)
                    env.mark(f"p3.t1.{i}", "vc")
                    env.mark(f"p3.xqfree.{i}", "vc")
                env.wait("v", f"p3.st2.{g_ * CC + CC - 1}")
                v.tensor_scalar_mul(st3[0:1, 0:512], pb[2][0:1, 0:512],
                                    1.0 / 2048)
                v.tensor_scalar_mul(st3[0:1, 512:1024], pb[3][0:1, 0:512],
                                    1.0 / 2048)
                v.tensor_tensor(st3[0:1, 1024:1536], st3[0:1, 0:512],
                                st3[0:1, 0:512], ALU.mult)
                v.tensor_tensor(st3[0:1, 1536:2048], st3[0:1, 512:1024],
                                st3[0:1, 1024:1536], ALU.subtract)
                v.tensor_scalar_add(st3[0:1, 1536:2048], st3[0:1, 1536:2048],
                                    1e-5).then_inc(vc, 1)
                env.mark(f"p3.var.{g_}", "vc")
                env.wait("v", f"p3.rstd.{g_}")
                v.tensor_tensor(st3[0:1, 2560:3072], st3[0:1, 0:512],
                                a_sb[0:1, 0:512], ALU.mult)
                v.tensor_scalar_mul(b_sb[0:1, 0:512], st3[0:1, 2560:3072],
                                    -1.0).then_inc(vc, 1)
                env.mark(f"p3.ab.{g_}", "vc")
                env.wait("v", f"p3.abbc.{g_}")
                v.tensor_copy(a_sb, pb[4])
                v.tensor_copy(b_sb, pb[5]).then_inc(vc, 1)
                env.mark(f"p3.absb.{g_}", "vc")
                for cc_ in range(CC):
                    i = g_ * CC + cc_
                    if i >= 2:
                        env.wait("v", f"p3.lnfout.{i - 2}")
                    v.tensor_tensor(w1b[i % 2],
                                    t1sb[:, cc_ * 512:(cc_ + 1) * 512],
                                    a_sb, ALU.mult)
                    v.tensor_tensor(lnf[i % 2], w1b[i % 2], b_sb, ALU.add
                                    ).then_inc(vc, 1)
                    env.mark(f"p3.lnf.{i}", "vc")
                    if i == NG * CC - 1:
                        env.mark("p3.alldone", "vc")
            for g_ in range(NG):
                for qd in range(16):
                    for s_ in range(4):
                        i = g_ * 64 + qd * 4 + s_
                        env.wait("v", f"p4.sg.{i}")
                        env.wait("v", f"p4.umm.{i}")
                        if g_ >= 1 and qd == 0 and s_ == 0:
                            env.wait("v", f"p4.dmm.{g_ * CC - 1}")
                        v.tensor_tensor(hq[s_], sg[i % 2], pb[2 + s_ % 2],
                                        ALU.mult).then_inc(vc, 1)
                        env.mark(f"p4.hqv.{i}", "vc")
                        env.mark(f"p4.sgfree.{i}", "vc")
                    env.wait("v", f"p4.habs.{g_}.{qd}")
                    a0, a1, a2, a3 = habs
                    v.tensor_tensor(dd[0], a1, a0, ALU.is_gt)
                    v.tensor_tensor(dd[1], a2, a0, ALU.is_gt)
                    v.tensor_tensor(dd[2], a3, a0, ALU.is_gt)
                    v.tensor_tensor(dd[3], a2, a1, ALU.is_gt)
                    v.tensor_tensor(dd[4], a3, a1, ALU.is_gt)
                    v.tensor_tensor(dd[5], a3, a2, ALU.is_gt)
                    v.tensor_tensor(ee[0], dd[0], dd[1], ALU.add)
                    v.tensor_tensor(ee[0], ee[0], dd[2], ALU.add)
                    v.tensor_single_scalar(kkm[0], ee[0], 1.5, ALU.is_le)
                    v.tensor_tensor(ee[1], dd[3], dd[4], ALU.add)
                    v.tensor_tensor(ee[1], ee[1], dd[0], ALU.subtract)
                    v.tensor_single_scalar(kkm[1], ee[1], 0.5, ALU.is_le)
                    v.tensor_tensor(ee[0], dd[1], dd[3], ALU.add)
                    v.tensor_tensor(ee[0], dd[5], ee[0], ALU.subtract)
                    v.tensor_single_scalar(kkm[2], ee[0], -0.5, ALU.is_le)
                    v.tensor_tensor(ee[1], dd[2], dd[4], ALU.add)
                    v.tensor_tensor(ee[1], ee[1], dd[5], ALU.add)
                    v.tensor_single_scalar(kkm[3], ee[1], 1.5, ALU.is_ge)
                    for s_ in range(4):
                        ig = s_ * 16 + qd
                        ins = v.tensor_tensor(hsp[:, ig * 512:(ig + 1) * 512],
                                              hq[s_], kkm[s_], ALU.mult)
                    ins.then_inc(vc, 1)
                    env.mark(f"p4.spq.{g_}.{qd}", "vc")
                env.mark(f"p4.hsp.{g_}", "vc")
                for cc_ in range(CC):
                    i = g_ * CC + cc_
                    env.wait("v", f"p4.dmm.{i}")
                    env.wait("v", f"p4.lnfb.{i}")
                    if i >= CC:
                        env.wait("v", f"p4.st1.{i - CC}")
                    v.tensor_tensor(t2sb[:, cc_ * 512:(cc_ + 1) * 512],
                                    pb[4 + i % 2], lnfb[i % 3], ALU.add
                                    ).then_inc(vc, 1)
                    env.mark(f"p4.t2v.{i}", "vc")
                    env.mark(f"p4.lnffree.{i}", "vc")
                env.wait("v", f"p4.st2.{g_ * CC + CC - 1}")
                v.tensor_scalar_mul(st4[0:1, 0:512], pb[6][0:1, 0:512],
                                    1.0 / 2048)
                v.tensor_scalar_mul(st4[0:1, 512:1024], pb[0][0:1, 0:512],
                                    1.0 / 2048)
                v.tensor_tensor(st4[0:1, 1024:1536], st4[0:1, 0:512],
                                st4[0:1, 0:512], ALU.mult)
                v.tensor_tensor(st4[0:1, 1536:2048], st4[0:1, 512:1024],
                                st4[0:1, 1024:1536], ALU.subtract
                                ).then_inc(vc, 1)
                env.mark(f"p4.var.{g_}", "vc")
                env.wait("v", f"p4.rstd.{g_}")
                v.tensor_tensor(st4[0:1, 2560:3072], st4[0:1, 0:512],
                                a4_sb[0:1, 0:512], ALU.mult)
                v.tensor_scalar_mul(b4_sb[0:1, 0:512], st4[0:1, 2560:3072],
                                    -1.0).then_inc(vc, 1)
                env.mark(f"p4.ab.{g_}", "vc")
                env.wait("v", f"p4.abbc.{g_}")
                v.tensor_copy(a4_sb, pb[0])
                v.tensor_copy(b4_sb, pb[1]).then_inc(vc, 1)
                env.mark(f"p4.absb.{g_}", "vc")
                for cc_ in range(CC):
                    i = g_ * CC + cc_
                    if i >= 2:
                        env.wait("v", f"p4.o2out.{i - 2}")
                    v.tensor_tensor(sq4[i % 2],
                                    t2sb[:, cc_ * 512:(cc_ + 1) * 512],
                                    a4_sb, ALU.mult)
                    v.tensor_tensor(o2[i % 2], sq4[i % 2], b4_sb, ALU.add
                                    ).then_inc(vc, 1)
                    env.mark(f"p4.o2.{i}", "vc")
                env.mark(f"p4.gvec.{g_}.done", "vc")

        mocks = {k: _MockEngine() for k in ("g", "sy", "t", "s", "v")}
        env.mock = True
        env.reset()
        emit_gpsimd(mocks["g"])
        emit_sync(mocks["sy"])
        emit_tensor(mocks["t"])
        emit_scalar(mocks["s"])
        emit_vector(mocks["v"])

        for s in env.sems.values():
            s.h = ctx.enter_context(nc.semaphore(s.name))
        env.mock = False
        env.reset()
        with nc.Block() as block:
            @block.gpsimd
            def _(g):
                emit_gpsimd(_RealEngine(g))

            @block.sync
            def _(sy):
                emit_sync(_RealEngine(sy))

            @block.tensor
            def _(t):
                emit_tensor(_RealEngine(t))

            @block.scalar
            def _(s):
                emit_scalar(_RealEngine(s))

            @block.vector
            def _(v):
                emit_vector(_RealEngine(v))

    return nc



def kernel(**inputs):
    in_maps, own_idx_all = _host_prep(**inputs)
    nc = build_program()
    res = run_bass_kernel_spmd(nc, in_maps, core_ids=list(range(NCORE)))
    out = np.empty((B, T, C), np.float32)
    for core in range(NCORE):
        b = core // 2
        co = np.asarray(res.results[core]["OUT"])
        rows = co.transpose(2, 0, 1).reshape(TOK, C)
        out[b, own_idx_all[core], :] = rows
    return out
